# version 13
# baseline (speedup 1.0000x reference)
import math
from contextlib import ExitStack

import numpy as np
import ml_dtypes

import concourse.bass as bass
import concourse.mybir as mybir
from concourse.bass_utils import run_bass_kernel_spmd
from concourse.expressions import RuntimeValue

F32 = mybir.dt.float32
BF16 = mybir.dt.bfloat16
I32 = mybir.dt.int32
AF = mybir.ActivationFunctionType
ALU = mybir.AluOpType
AX = mybir.AxisListType
ds = bass.ds

B, S, D = 4, 2048, 2048
H, D_H, D_HR, D_LAT = 16, 128, 64, 512
NT = S // 128
NO = 8
NE = 7
D_QK = D_H + D_HR
SQRT_DQK_F32 = float(np.float32(np.sqrt(np.float32(float(D_QK)))))
SCALE_QK = 1.0 / SQRT_DQK_F32
CLIP = 80.0
MASK_NEG = -100000.0
LN_EPS = 1e-5
CLAMP_HI = math.exp(CLIP / SQRT_DQK_F32)
CLAMP_LO = math.exp(-CLIP / SQRT_DQK_F32)

OWN_EVEN = [0, 15, 3, 12, 4, 11, 7, 8]
OWN_ODD = [1, 14, 2, 13, 5, 10, 6, 9]
PA = [max(a, b) + 1 for a, b in zip(OWN_EVEN, OWN_ODD)]
NSLOT = sum(PA)

SIN_C = [0.9999999999999537, -0.1666666666647968, 0.008333333320897147,
         -0.00019841266760649062, 2.7556957552677794e-06,
         -2.5030378131205352e-08, 1.5411754176956953e-10]
TWO_PI = 2.0 * math.pi
PI = math.pi
INVF = [10000.0 ** (-k / 32.0) for k in range(32)]
C2PI_1 = 6.283203125
C2PI_2 = -1.7814338207244873e-05
C2PI_3 = -3.482206523131026e-09

ENGINES = ("sync", "gpsimd", "tensor", "vector", "scalar")


class _EndBuild(Exception):
    pass


class Prog:

    def __init__(self):
        self.steps = []
        self.n = {}
        self.mark = {}

    def v(self, s):
        return self.n.get(s, 0)

    def step(self, eng, fn=None, wait=(), inc=()):
        for s, a in inc:
            self.n[s] = self.n.get(s, 0) + a
        self.steps.append((eng, fn, tuple(wait), tuple(inc)))

    def dma(self, eng, fn, wait=(), sem="w"):
        self.step(eng, fn, wait, ((sem, 16),))
        return self.v(sem)

    def c(self, eng, fn, wait=(), sem=None):
        inc = ((sem, 1),) if sem else ()
        self.step(eng, fn, wait, inc)
        return self.v(sem) if sem else None

    def emit(self, name, eng, sems, marks=None):
        nc_b = getattr(eng, "bass", None)
        cur = None
        for idx, (e, fn, wait, inc) in enumerate(self.steps):
            if marks is not None:
                ph = marks(idx)
                if ph != cur and nc_b is not None:
                    if cur is not None:
                        nc_b.pop_named_scope(cur)
                    if ph is not None:
                        nc_b.push_named_scope(ph)
                    cur = ph
            if e != name:
                continue
            for s, v in wait:
                eng.wait_ge(sems[s], v)
            ins = fn(eng) if fn is not None else None
            if inc:
                assert ins is not None and hasattr(ins, "then_inc"), (fn, inc)
                assert len(inc) == 1
                (s0, a0) = inc[0]
                ins.then_inc(sems[s0], a0)
        if marks is not None and cur is not None and nc_b is not None:
            nc_b.pop_named_scope(cur)


def build_nc(debug_taps=(), end_at="full", a1_stages=("xn", "ckv", "kr", "kc", "v"), a1_groups=4, pre_stages=("trig", "wtrans")):
    nc = bass.Bass("TRN2", target_bir_lowering=False, detect_race_conditions=False)

    x_h = nc.dram_tensor("x", [S, D], F32, kind="ExternalInput").ap()
    w = {}
    for name, shape in [
        ("W_DQ", [D_LAT, D]), ("W_UQ", [H * D_H, D_LAT]), ("W_QR", [H * D_HR, D_LAT]),
        ("W_DKV", [D_LAT, D]), ("W_UK", [H * D_H, D_LAT]), ("W_UV", [H * D_H, D_LAT]),
        ("W_KR", [D_HR, D]), ("W_O", [D, H * D_H]),
        ("W1", [NE, D, D]), ("Wg", [NE, D, D]), ("W2", [NE, D, D]),
        ("Ws1", [D, D]), ("Wsg", [D, D]), ("Ws2", [D, D]), ("gate_w", [D, NE]),
    ]:
        w[name] = nc.dram_tensor(name, shape, F32, kind="ExternalInput").ap()
    gate_b_h = nc.dram_tensor("gate_b_rep", [128, NE], F32, kind="ExternalInput").ap()
    biases_h = nc.dram_tensor("biases_rep", [128, NE], F32, kind="ExternalInput").ap()
    ident_h = nc.dram_tensor("ident", [128, 128], BF16, kind="ExternalInput").ap()
    maskb_h = nc.dram_tensor("mask_band", [128, 16 * 128], BF16, kind="ExternalInput").ap()
    zpad_h = nc.dram_tensor("zpad", [64, 16 * 128], BF16, kind="ExternalInput").ap()
    sin_th = nc.dram_tensor("sin_t", [128, 512], F32, kind="ExternalInput").ap()
    cos_th = nc.dram_tensor("cos_t", [128, 512], F32, kind="ExternalInput").ap()
    pos_h = nc.dram_tensor("pos", [128, NT], F32, kind="ExternalInput").ap()
    ownj_h = nc.dram_tensor("own_j", [1, NO], I32, kind="ExternalInput").ap()

    y_h = nc.dram_tensor("y", [NO, 128, D], F32, kind="ExternalOutput").ap()

    xn_scr = nc.dram_tensor("xn_scr", [NT, 128, D], BF16, kind="Internal").ap()
    kc_scr = nc.dram_tensor("kc_scr", [H, NT, 128, 128], BF16, kind="Internal").ap()
    qc_scr = nc.dram_tensor("qc_scr", [NO, H, 128, 128], BF16, kind="Internal").ap()
    qr_scr = nc.dram_tensor("qr_scr", [NO, H, 64, 128], BF16, kind="Internal").ap()
    op_scr = nc.dram_tensor("op_scr", [NO, 128, D], F32, kind="Internal").ap()
    scr = {"xn_scr": xn_scr, "kc_scr": kc_scr, "qc_scr": qc_scr,
           "qr_scr": qr_scr, "op_scr": op_scr}
    tap_shapes = {"xn_scr": ([NT, 128, D], BF16), "kc_scr": ([H, NT, 128, 128], BF16),
                  "qc_scr": ([NO, H, 128, 128], BF16), "qr_scr": ([NO, H, 64, 128], BF16),
                  "op_scr": ([NO, 128, D], F32),
                  "v_sb": ([128, NT * 2048], BF16), "kr_sb": ([128, S], BF16),
                  "sin_sb": ([128, 512], F32), "cos_sb": ([128, 512], F32)}
    taps = {t: nc.dram_tensor("tap_" + t, tap_shapes[t][0], tap_shapes[t][1],
                              kind="ExternalOutput").ap() for t in debug_taps}

    p = Prog()
    ctx = ExitStack()
    with ctx:
      _open = []
      try:
        E = ctx.enter_context

        ident = E(nc.sbuf_tensor("ident_sb", [128, 128], BF16))
        pos_sb = E(nc.sbuf_tensor("pos_sb", [128, NT], F32))
        ownj = E(nc.sbuf_tensor("ownj_sb", [1, NO], I32))
        gateb = E(nc.sbuf_tensor("gateb_sb", [128, NE], F32))
        biase = E(nc.sbuf_tensor("biase_sb", [128, NE], F32))
        abctx = ExitStack()
        _open.append(abctx)
        AB = abctx.enter_context
        sin_sb = AB(nc.sbuf_tensor("sin_sb", [128, 512], F32))
        cos_sb = AB(nc.sbuf_tensor("cos_sb", [128, 512], F32))
        v_sb = AB(nc.sbuf_tensor("v_sb", [128, NT * 2048], BF16))
        kr_sb = AB(nc.sbuf_tensor("kr_sb", [128, S], BF16))

        tr1 = AB(nc.sbuf_tensor("tr1", [128, 512], F32))
        tr2 = AB(nc.sbuf_tensor("tr2", [128, 512], F32))

        ps = [E(nc.psum_tensor(f"ps{i}", [128, 512], F32)) for i in range(6)]
        psb = [E(nc.psum_tensor(f"psb{i}", [128, 512], BF16)) for i in range(2)]

        sems = {n: E(nc.semaphore("sem_" + n)) for n in
                ["x", "w", "t", "v", "s", "q", "o", "w2"]}

        block = E(nc.Block())
        _regs = {}

        def j_rv(e, i):
            key = (e.engine, "j", i)
            if key not in _regs:
                r = e.alloc_register(f"j{i}")
                e.reg_load(r, ownj[0:1, i:i + 1])
                _regs[key] = r
            return RuntimeValue(_regs[key], 0, 15)

        def jx_rv(e, i):
            key = (e.engine, "jx", i)
            if key not in _regs:
                r = e.alloc_register(f"jx{i}")
                e.reg_load(r, ownj[0:1, i:i + 1])
                e.reg_mul(r, r, 128)
                _regs[key] = r
            return RuntimeValue(_regs[key], 0, 1920)


        for dst, src in [(sin_sb, sin_th), (cos_sb, cos_th), (ident, ident_h),
                         (pos_sb, pos_h), (ownj, ownj_h), (gateb, gate_b_h),
                         (biase, biases_h)]:
            p.dma("sync", lambda e, d=dst, s=src: e.dma_start(out=d[:], in_=s[:]), sem="q")
        consts_q = p.v("q")

        trig_v = 0

        actx = ExitStack()
        _open.append(actx)
        AC = actx.enter_context
        wdkv = AC(nc.sbuf_tensor("wdkv", [128, 16 * 512], BF16))
        wuk = AC(nc.sbuf_tensor("wuk", [128, 4 * 2048], BF16))
        wuv = AC(nc.sbuf_tensor("wuv", [128, 4 * 2048], BF16))
        wkr = AC(nc.sbuf_tensor("wkr", [128, 16 * 64], BF16))
        wnat = AC(nc.sbuf_tensor("wnat", [128, 8192], BF16))

        def wtrans(nat_h, nrows, ncols, dst_sb, wnat):
            rb = max(1, nrows // 128)
            prows = min(nrows, 128)
            cb = ncols // 128
            piece = max(1, min(rb, 8192 // ncols))
            for b0 in range(0, rb, piece):
                bn = min(piece, rb - b0)
                nat_ap = nat_h[b0 * 128:(b0 * 128 + bn * prows), :]
                if nrows >= 128:
                    nat_ap = nat_ap.rearrange("(b q) n -> q b n", q=128)
                else:
                    nat_ap = nat_ap.unsqueeze(1)
                wq = p.dma("gpsimd", lambda e, a=nat_ap, bn=bn: e.dma_start(
                    out=wnat[0:prows, 0:bn * ncols].rearrange("q (b n) -> q b n", b=bn), in_=a),
                    wait=(("t", p.v("t")),))
                tiles = [(a, b1) for a in range(cb) for b1 in range(bn)]
                v_base = p.v("v")
                for nb0 in range(0, len(tiles), 4):
                    batch = tiles[nb0:nb0 + 4]
                    bank = (nb0 // 4) % 2
                    if nb0 == 0:
                        wt = (("w", wq), ("v", v_base))
                    elif nb0 >= 8:
                        wt = (("v", p.v("v") - 1),)
                    else:
                        wt = ()
                    p.c("tensor", lambda t, batch=batch, bank=bank: [t.transpose(
                        psb[bank][0:128, k * 128:k * 128 + prows],
                        wnat[0:prows, b1 * ncols + a * 128: b1 * ncols + a * 128 + 128],
                        ident[0:prows, 0:prows]) for k, (a, b1) in enumerate(batch)][-1],
                        wait=wt, sem="t")
                    if prows == 128 and all(
                            (batch[k][0] == batch[0][0] and batch[k][1] == batch[0][1] + k)
                            for k in range(len(batch))):
                        a0, b10 = batch[0]
                        p.c("vector", lambda v, a0=a0, b10=b10, b0=b0, bank=bank, nkb=len(batch): v.tensor_copy(
                            dst_sb[:, a0 * nrows + (b0 + b10) * 128:
                                   a0 * nrows + (b0 + b10) * 128 + nkb * 128],
                            psb[bank][0:128, 0:nkb * 128]),
                            wait=(("t", p.v("t")),), sem="v")
                    else:
                        p.c("vector", lambda v, batch=batch, bank=bank, b0=b0: [v.tensor_copy(
                            dst_sb[:, a * nrows + (b0 + b1) * 128: a * nrows + (b0 + b1) * 128 + prows],
                            psb[bank][0:128, k * 128:k * 128 + prows]) for k, (a, b1) in enumerate(batch)][-1],
                            wait=(("t", p.v("t")),), sem="v")
            return p.v("v")

        p.step("gpsimd", None, wait=(("q", consts_q),))
        p.step("tensor", None, wait=(("q", consts_q),))
        if "wtrans" in pre_stages:
            wtrans(w["W_DKV"], D_LAT, D, wdkv, wnat)
            wtrans(w["W_UK"], H * D_H, D_LAT, wuk, wnat)
            wtrans(w["W_UV"], H * D_H, D_LAT, wuv, wnat)
            kvw_v = wtrans(w["W_KR"], D_HR, D, wkr, wnat)
        else:
            kvw_v = p.v("v")

        p.dma("sync", lambda e: e.dma_start(out=kr_sb[64:128, :], in_=zpad_h[:]), sem="q")
        x_bufs = [AC(nc.sbuf_tensor(f"xb{i}", [128, D], F32)) for i in range(2)]
        xn_tm = AC(nc.sbuf_tensor("xn_tm", [128, D], BF16))
        xc_sb = AC(nc.sbuf_tensor("xc_sb", [128, D], F32))
        sq_sb = AC(nc.sbuf_tensor("sq_sb", [128, D], F32))
        ln_m = AC(nc.sbuf_tensor("ln_m", [128, 8], F32))
        xn_fm = AC(nc.sbuf_tensor("xn_fm", [128, 16 * 512], BF16))
        ckv = AC(nc.sbuf_tensor("ckv", [128, 4 * 512], BF16))
        ev0 = AC(nc.sbuf_tensor("ev0", [128, 512], BF16))
        ev1 = AC(nc.sbuf_tensor("ev1", [128, 512], BF16))
        krst = AC(nc.sbuf_tensor("krst", [128, 64], BF16))

        for t in range(2):
            p.dma("sync", lambda e, t=t: e.dma_start(out=x_bufs[t][:], in_=x_h[t * 128:(t + 1) * 128, :]),
                  sem="x")

        kc_dma_w = []
        xn_dma_w = {}

        for g in range(a1_groups):
            for ti in range(4):
                t = g * 4 + ti
                xb = x_bufs[t % 2]
                p.c("vector", lambda v, xb=xb: (
                    v.reduce_sum(ln_m[:, 0:1], xb[:], axis=AX.X),
                    v.drain(),
                    v.tensor_scalar_mul(ln_m[:, 1:2], ln_m[:, 0:1], 1.0 / D),
                    v.drain(),
                    v.tensor_scalar(xc_sb[:], xb[:], ln_m[:, 1:2], None, ALU.subtract))[-1],
                    wait=(("x", 16 * (t + 1)),), sem="v")
                p.c("scalar", lambda sc: sc.activation(
                    sq_sb[:], xc_sb[:], AF.Square, accum_out=ln_m[:, 2:3]),
                    wait=(("v", p.v("v")),), sem="s")
                p.c("vector", lambda v: v.tensor_scalar(
                    ln_m[:, 3:4], ln_m[:, 2:3], 1.0 / D, LN_EPS, ALU.mult, ALU.add),
                    wait=(("s", p.v("s")),), sem="v")
                p.c("scalar", lambda sc: sc.activation(ln_m[:, 4:5], ln_m[:, 3:4], AF.Sqrt),
                    wait=(("v", p.v("v")),), sem="s")
                p.c("vector", lambda v: (
                    v.reciprocal(ln_m[:, 5:6], ln_m[:, 4:5]),
                    v.drain(),
                    v.tensor_scalar(xn_tm[:], xc_sb[:], ln_m[:, 5:6], None, ALU.mult))[-1],
                    wait=(("s", p.v("s")), ("t", p.v("t"))), sem="v")
                xn_tm_v = p.v("v")
                if t + 2 < NT:
                    p.dma("sync", lambda e, t=t: e.dma_start(
                        out=x_bufs[t % 2][:], in_=x_h[(t + 2) * 128:(t + 3) * 128, :]),
                        wait=(("v", xn_tm_v),), sem="x")
                for kbatch in range(4):
                    wt = [("v", xn_tm_v)]
                    if kbatch == 0 and (t - 4) in xn_dma_w:
                        wt.append(("w", xn_dma_w[t - 4]))
                    p.c("tensor", lambda tn, kb=kbatch: [
                        tn.transpose(psb[kb % 2][:, q * 128:q * 128 + 128],
                                     xn_tm[:, (kb * 4 + q) * 128:(kb * 4 + q) * 128 + 128],
                                     ident[:]) for q in range(4)][-1],
                        wait=tuple(wt) + ((("v", p.v("v")),) if kbatch >= 2 else ()),
                        sem="t")
                    p.c("vector", lambda v, kb=kbatch, ti=ti: [
                        v.tensor_copy(xn_fm[:, (kb * 4 + q) * 512 + ti * 128:(kb * 4 + q) * 512 + ti * 128 + 128],
                                      psb[kb % 2][:, q * 128:q * 128 + 128]) for q in range(4)][-1],
                        wait=(("t", p.v("t")),), sem="v")
                xn_dma_w[t] = p.dma("gpsimd", lambda e, t=t, ti=ti: e.dma_start(
                    out=xn_scr[t].rearrange("q (dc n) -> q dc n", dc=16),
                    in_=xn_fm[:].rearrange("q (dc n) -> q dc n", dc=16)[:, :, ti * 128:ti * 128 + 128]),
                    wait=(("v", p.v("v")),))

            xn_group_v = p.v("v")
            if "ckv" not in a1_stages:
                continue
            for lc in range(4):
                wt = [("v", xn_group_v), ("w", p.v("w"))] if lc == 0 else \
                     ([("v", p.v("v"))] if lc >= 2 else [])
                p.c("tensor", lambda tn, lc=lc: [
                    tn.matmul(ps[lc % 2][:, 0:512],
                              wdkv[:, dc * 512 + lc * 128: dc * 512 + lc * 128 + 128],
                              xn_fm[:, dc * 512: dc * 512 + 512],
                              start=(dc == 0), stop=(dc == 15)) for dc in range(16)][-1],
                    wait=tuple(wt), sem="t")
                p.c("vector", lambda v, lc=lc: v.tensor_copy(
                    ckv[:, lc * 512:(lc + 1) * 512], ps[lc % 2][:, 0:512]),
                    wait=(("t", p.v("t")),), sem="v")
            ckv_v = p.v("v")
            if "kr" not in a1_stages:
                continue
            for ti in range(4):
                t = g * 4 + ti
                p.c("tensor", lambda tn, ti=ti: [
                    tn.matmul(ps[2][0:128, 0:64],
                              xn_fm[:, dc * 512 + ti * 128: dc * 512 + ti * 128 + 128],
                              wkr[:, dc * 64:(dc + 1) * 64],
                              start=(dc == 0), stop=(dc == 15)) for dc in range(16)][-1],
                    wait=((("v", p.v("v")),) if ti > 0 else (("v", ckv_v),)), sem="t")

                def rope_kr(v, t=t):
                    x1 = ps[2][:, 0:64:2]
                    x2 = ps[2][:, 1:64:2]
                    cosn = cos_sb[:, t:512:16]
                    sinn = sin_sb[:, t:512:16]
                    a = tr1[:, 0:32]
                    b = tr2[:, 0:32]
                    v.tensor_mul(a, x1, cosn)
                    v.tensor_mul(b, x2, sinn)
                    v.drain()
                    v.tensor_sub(krst[:, 0:64:2], a, b)
                    v.tensor_mul(a, x1, sinn)
                    v.tensor_mul(b, x2, cosn)
                    v.drain()
                    return v.tensor_add(krst[:, 1:64:2], a, b)
                p.c("vector", rope_kr, wait=(("t", p.v("t")),), sem="v")
                p.c("tensor", lambda tn: tn.transpose(psb[0][0:64, 0:128], krst[:], ident[:]),
                    wait=(("v", p.v("v")),), sem="t")
                p.c("vector", lambda v, t=t: v.tensor_copy(
                    kr_sb[0:64, t * 128:(t + 1) * 128], psb[0][0:64, 0:128]),
                    wait=(("t", p.v("t")),), sem="v")

            if "kc" not in a1_stages:
                continue
            for h in range(16):
                wt = [("v", p.v("v"))] if h < 2 else [("v", p.v("v") - 1)]
                p.c("tensor", lambda tn, h=h: [
                    tn.matmul(ps[2 + h % 2][:, 0:512],
                              wuk[:, lc * 2048 + h * 128: lc * 2048 + h * 128 + 128],
                              ckv[:, lc * 512: lc * 512 + 512],
                              start=(lc == 0), stop=(lc == 3)) for lc in range(4)][-1],
                    wait=tuple(wt), sem="t")
                evw = [("t", p.v("t"))]
                if len(kc_dma_w) >= 2:
                    evw.append(("w", kc_dma_w[-2]))
                p.c("vector", lambda v, h=h: v.tensor_copy(
                    (ev0 if h % 2 == 0 else ev1)[:], ps[2 + h % 2][:, 0:512]),
                    wait=tuple(evw), sem="v")
                kc_dma_w.append(p.dma("gpsimd", lambda e, h=h, g=g: e.dma_start(
                    out=kc_scr[h, g * 4:(g + 1) * 4].rearrange("t q n -> q t n"),
                    in_=(ev0 if h % 2 == 0 else ev1)[:].rearrange("q (t n) -> q t n", t=4)),
                    wait=(("v", p.v("v")),)))

            if "v" not in a1_stages:
                continue
            for ti in range(4):
                t = g * 4 + ti
                for a in range(4):
                    idx = ti * 4 + a
                    p.c("tensor", lambda tn, ti=ti, a=a, idx=idx: [
                        tn.matmul(ps[4 + idx % 2][:, 0:512],
                                  ckv[:, lc * 512 + ti * 128: lc * 512 + ti * 128 + 128],
                                  wuv[:, lc * 2048 + a * 512: lc * 2048 + a * 512 + 512],
                                  start=(lc == 0), stop=(lc == 3)) for lc in range(4)][-1],
                        wait=(("v", p.v("v") - (1 if idx >= 2 else 0)),), sem="t")
                    p.c("vector", lambda v, t=t, a=a, idx=idx: v.tensor_copy(
                        v_sb[:, t * 2048 + a * 512: t * 2048 + a * 512 + 512],
                        ps[4 + idx % 2][:, 0:512]),
                        wait=(("t", p.v("t")),), sem="v")

        a1_v = p.v("v")
        a1_w = p.v("w")

        def tap_out(names, wv, wq=None):
            for tname in names:
                if tname not in taps:
                    continue
                tap = taps[tname]
                if tname in ("v_sb", "kr_sb", "sin_sb", "cos_sb"):
                    smap = {"v_sb": v_sb, "kr_sb": kr_sb, "sin_sb": sin_sb, "cos_sb": cos_sb}
                    p.dma("sync", lambda e, s=smap[tname], d=tap: e.dma_start(out=d[:], in_=s[:]),
                          wait=wv, sem="o")
                else:
                    s = scr[tname]
                    p.dma("sync", lambda e, s=s, d=tap: e.dma_start(
                        out=d.flatten().rearrange("(a b) -> a b", a=128),
                        in_=s.flatten().rearrange("(a b) -> a b", a=128)),
                        wait=wv, sem="o")
        tap_out(("xn_scr", "kc_scr", "v_sb", "kr_sb", "sin_sb", "cos_sb"),
                (("v", a1_v), ("w", a1_w)))

        _open.remove(actx); actx.close()
        if end_at == "A1":
            raise _EndBuild
        a2ctx = ExitStack()
        _open.append(a2ctx)
        A2 = a2ctx.enter_context
        wdq = A2(nc.sbuf_tensor("wdq", [128, 16 * 512], BF16))
        wuq = A2(nc.sbuf_tensor("wuq", [128, 4 * 2048], BF16))
        wqr = A2(nc.sbuf_tensor("wqr", [128, 4 * 1024], BF16))
        wnat2 = A2(nc.sbuf_tensor("wnat2", [128, 8192], BF16))
        xno = A2(nc.sbuf_tensor("xno", [128, 16 * 1024], BF16))
        cq = A2(nc.sbuf_tensor("cq", [128, 4 * 1024], BF16))
        qstg = A2(nc.sbuf_tensor("qstg", [128, 1024], BF16))
        qrtm = A2(nc.sbuf_tensor("qrtm", [128, 1024], BF16))
        qrstage = A2(nc.sbuf_tensor("qrstage", [128, 1024], BF16))
        cos_own = A2(nc.sbuf_tensor("cos_own", [128, 256], F32))
        sin_own = A2(nc.sbuf_tensor("sin_own", [128, 256], F32))

        def trig_gather(e):
            last = None
            for i in range(NO):
                rv = j_rv(e, i)
                for dst, srct in ((cos_own, cos_sb), (sin_own, sin_sb)):
                    last = e.tensor_copy(
                        dst[:, i * 32:(i + 1) * 32],
                        srct[:].rearrange("p (k t) -> p k t", t=16)[:, :, ds(rv, 1)].squeeze(2))
            return last
        p.c("gpsimd", trig_gather, wait=(("q", consts_q),), sem="w2")
        trig_own_w = p.v("w2")

        wtrans(w["W_DQ"], D_LAT, D, wdq, wnat2)
        wtrans(w["W_UQ"], H * D_H, D_LAT, wuq, wnat2)
        a2w_v = wtrans(w["W_QR"], H * D_HR, D_LAT, wqr, wnat2)

        def mk_xno_dma(i):
            def f(e):
                rv = j_rv(e, i)
                return e.dma_start(
                    out=xno[:].rearrange("p (dc n) -> p dc n", n=1024)[:, :, i * 128:(i + 1) * 128],
                    in_=xn_scr[ds(rv, 1)].squeeze(0).rearrange("p (dc n) -> p dc n", dc=16))
            return f
        for i in range(NO):
            p.dma("sync", mk_xno_dma(i),
                  wait=((("w", a1_w), ("q", consts_q)) if i == 0 else ()), sem="q")
        xno_q = p.v("q")

        for lc in range(4):
            for half in range(2):
                idx = lc * 2 + half
                wt = [("q", xno_q), ("v", a2w_v)] if idx == 0 else \
                     ([("v", p.v("v"))] if idx >= 2 else [])
                p.c("tensor", lambda tn, lc=lc, half=half, idx=idx: [
                    tn.matmul(ps[idx % 2][:, 0:512],
                              wdq[:, dc * 512 + lc * 128: dc * 512 + lc * 128 + 128],
                              xno[:, dc * 1024 + half * 512: dc * 1024 + half * 512 + 512],
                              start=(dc == 0), stop=(dc == 15)) for dc in range(16)][-1],
                    wait=tuple(wt), sem="t")
                p.c("vector", lambda v, lc=lc, half=half, idx=idx: v.tensor_copy(
                    cq[:, lc * 1024 + half * 512: lc * 1024 + half * 512 + 512],
                    ps[idx % 2][:, 0:512]), wait=(("t", p.v("t")),), sem="v")
        cq_v = p.v("v")

        qc_dma_w = []
        for h in range(16):
            for half in range(2):
                wt = [("v", cq_v)] if h == 0 and half == 0 else [("v", p.v("v"))]
                p.c("tensor", lambda tn, h=h, half=half: [
                    tn.matmul(ps[2 + half][:, 0:512],
                              wuq[:, lc * 2048 + h * 128: lc * 2048 + h * 128 + 128],
                              cq[:, lc * 1024 + half * 512: lc * 1024 + half * 512 + 512],
                              start=(lc == 0), stop=(lc == 3)) for lc in range(4)][-1],
                    wait=tuple(wt), sem="t")
                evw = [("t", p.v("t"))]
                if h >= 1 and half == 0:
                    evw.append(("w", qc_dma_w[h - 1]))
                p.c("vector", lambda v, half=half: v.tensor_copy(
                    qstg[:, half * 512:(half + 1) * 512], ps[2 + half][:, 0:512]),
                    wait=tuple(evw), sem="v")
            qc_dma_w.append(p.dma("gpsimd", lambda e, h=h: e.dma_start(
                out=qc_scr[:, h].rearrange("s p n -> p s n"),
                in_=qstg[:].rearrange("p (s n) -> p s n", s=8)),
                wait=(("v", p.v("v")),)))

        qr_dma_w = []
        for i in range(NO):
            for half in range(2):
                wt = [("v", p.v("v"))]
                p.c("tensor", lambda tn, i=i, half=half: [
                    tn.matmul(ps[4 + half][:, 0:512],
                              cq[:, lc * 1024 + i * 128: lc * 1024 + i * 128 + 128],
                              wqr[:, lc * 1024 + half * 512: lc * 1024 + half * 512 + 512],
                              start=(lc == 0), stop=(lc == 3)) for lc in range(4)][-1],
                    wait=tuple(wt), sem="t")

            def rope_qr(v, i=i):
                    cosn = cos_own[:, i * 32:(i + 1) * 32].unsqueeze(1).broadcast_to([128, 8, 32])
                    sinn = sin_own[:, i * 32:(i + 1) * 32].unsqueeze(1).broadcast_to([128, 8, 32])
                    last = None
                    for half in range(2):
                        pp = ps[4 + half]
                        x1 = pp[:, 0:512:2].rearrange("p (h n) -> p h n", h=8)
                        x2 = pp[:, 1:512:2].rearrange("p (h n) -> p h n", h=8)
                        a3 = tr1[:, 0:256].rearrange("p (h n) -> p h n", h=8)
                        b3 = tr2[:, 0:256].rearrange("p (h n) -> p h n", h=8)
                        dsts = qrtm[:, half * 512:(half + 1) * 512]
                        de = dsts[:, 0:512:2].rearrange("p (h n) -> p h n", h=8)
                        do = dsts[:, 1:512:2].rearrange("p (h n) -> p h n", h=8)
                        v.tensor_mul(a3, x1, cosn)
                        v.tensor_mul(b3, x2, sinn)
                        v.drain()
                        v.tensor_sub(de, a3, b3)
                        v.tensor_mul(a3, x1, sinn)
                        v.tensor_mul(b3, x2, cosn)
                        v.drain()
                        last = v.tensor_add(do, a3, b3)
                    return last
            p.c("vector", rope_qr, wait=(("t", p.v("t")), ("w2", trig_own_w)), sem="v")
            for c in range(8):
                wt = [("v", p.v("v"))] if c >= 2 or c == 0 else []
                if c == 0 and i > 0:
                    wt.append(("w", qr_dma_w[i - 1]))
                p.c("tensor", lambda tn, c=c: tn.transpose(
                    psb[c % 2][:, (c // 2 % 4) * 128:(c // 2 % 4) * 128 + 128],
                    qrtm[:, c * 128:(c + 1) * 128], ident[:]),
                    wait=tuple(wt), sem="t")
                p.c("vector", lambda v, c=c: v.tensor_copy(
                    qrstage[:, c * 128:(c + 1) * 128],
                    psb[c % 2][:, (c // 2 % 4) * 128:(c // 2 % 4) * 128 + 128]),
                    wait=(("t", p.v("t")),), sem="v")
            p.dma("gpsimd", lambda e, i=i: e.dma_start(
                out=qr_scr[i, 0:16:2].rearrange("h p n -> p h n"),
                in_=qrstage[0:64, :].rearrange("p (h n) -> p h n", h=8)),
                wait=(("v", p.v("v")),))
            qr_dma_w.append(p.dma("gpsimd", lambda e, i=i: e.dma_start(
                out=qr_scr[i, 1:16:2].rearrange("h p n -> p h n"),
                in_=qrstage[64:128, :].rearrange("p (h n) -> p h n", h=8)),
                wait=(("v", p.v("v")),)))
        a2_w = p.v("w")
        a2_v = p.v("v")
        tap_out(("qc_scr", "qr_scr"), (("w", a2_w), ("v", a2_v)))
        p.mark["A2e"] = len(p.steps)
        _open.remove(a2ctx); a2ctx.close()
        if end_at == "A2":
            raise _EndBuild
        bctx = ExitStack()
        _open.append(bctx)
        BC = bctx.enter_context
        wo = BC(nc.sbuf_tensor("wo", [128, 16 * 2048], BF16))
        w3ctx = ExitStack()
        wnat3 = w3ctx.enter_context(nc.sbuf_tensor("wnat3", [128, 8192], BF16))
        wtrans(w["W_O"], D, H * D_H, wo, wnat3)
        wo_v = p.v("v")
        w3ctx.close()
        qcb = [BC(nc.sbuf_tensor(f"qcb{i}", [128, 16 * 128], BF16)) for i in range(2)]
        qrb = [BC(nc.sbuf_tensor(f"qrb{i}", [128, 16 * 128], BF16)) for i in range(2)]
        kcb = [BC(nc.sbuf_tensor(f"kcb{i}", [128, 16 * 128], BF16)) for i in range(2)]
        prtm = [BC(nc.sbuf_tensor(f"prtm{i}", [128, 512], BF16)) for i in range(4)]
        prT = [BC(nc.sbuf_tensor(f"prT{i}", [128, 512], BF16)) for i in range(4)]
        expb = [BC(nc.sbuf_tensor(f"expb{i}", [128, 512], F32)) for i in range(4)]
        attn_tm = BC(nc.sbuf_tensor("attn_tm", [128, 2048], BF16))
        attn_fm = BC(nc.sbuf_tensor("attn_fm", [128, 2048], BF16))
        rsb = BC(nc.sbuf_tensor("rsb", [128, 160], F32))
        opstg = BC(nc.sbuf_tensor("opstg", [128, D], F32))
        maskb = BC(nc.sbuf_tensor("maskb_sb", [128, 16 * 128], BF16))
        p.dma("sync", lambda e: e.dma_start(out=maskb[:], in_=maskb_h[:]), sem="q")

        def mk_slab_dma(i, buf, scratch, stride_h):
            def f(e):
                return e.dma_start(out=buf[:, 0:stride_h * 16] if False else buf[:],
                                   in_=scratch[i].rearrange("h p n -> p h n"))
            return f

        slot_base = [sum(PA[:i]) for i in range(NO)]
        t_scores_last = {}
        b_q_q = []
        for i in range(NO):
            wtq = [("v", wo_v)] if i == 0 else ([("t", t_scores_last[(i - 2, 15)])] if i >= 2 else [])
            p.dma("sync", lambda e, i=i: e.dma_start(
                out=qcb[i % 2][:], in_=qc_scr[i].rearrange("h p n -> p h n")),
                wait=tuple(wtq), sem="q")
            p.dma("sync", lambda e, i=i: e.dma_start(
                out=qrb[i % 2][0:64, :], in_=qr_scr[i].rearrange("h p n -> p h n")),
                sem="q")
            if i < 2:
                p.dma("sync", lambda e, i=i: e.dma_start(
                    out=qrb[i % 2][64:128, :], in_=zpad_h[:]), sem="q")
            qslab_q = p.v("q")

            kc_q = {}
            epi_v = {}
            x_tr_u = {}
            v_clamp_u = {}
            t_av_u = {}
            lo_j = min(OWN_EVEN[i], OWN_ODD[i])
            g_band = lo_j // 4
            ng = (PA[i] + 3) // 4
            units = [(h, g2, min(4, PA[i] - 4 * g2))
                     for h in range(H) for g2 in range(ng)]
            SKEW = 2

            def emit_front(u, i=i, ng=ng, lo_j=lo_j, g_band=g_band, qslab_q=qslab_q):
                h, g2, nk = units[u]
                gq = u % 4
                pss = (ps[0], ps[1], ps[4], ps[5])[gq]
                if g2 == 0:
                    wtk = []
                    if h >= 2:
                        wtk.append(("t", t_scores_last[(i, h - 2)]))
                    elif i >= 1 and h < 2:
                        wtk.append(("t", t_scores_last[(i - 1, 14 + h)]))
                    p.dma("sync", lambda e, i=i, h=h: e.dma_start(
                        out=kcb[h % 2][:, 0:PA[i] * 128].rearrange("p (t n) -> p t n", n=128),
                        in_=kc_scr[h, 0:PA[i]].rearrange("t p n -> p t n")),
                        wait=tuple(wtk), sem="q")
                    kc_q[h] = p.v("q")
                wt = []
                if g2 == 0:
                    wt += [("q", kc_q[h]), ("q", qslab_q)]
                    if (i, h - 2) in epi_v:
                        wt.append(("v", epi_v[(i, h - 2)]))
                    elif h < 2 and (i - 1, 14 + h) in epi_v:
                        wt.append(("v", epi_v[(i - 1, 14 + h)]))
                    if h == 0 and i == 0:
                        wt.append(("v", wo_v))
                p.c("tensor", lambda tn, i=i, h=h, g2=g2, nk=nk, pss=pss: (
                    tn.matmul(pss[:, 0:nk * 128],
                              qcb[i % 2][:, h * 128:(h + 1) * 128],
                              kcb[h % 2][:, g2 * 512:g2 * 512 + nk * 128],
                              start=True, stop=False),
                    tn.matmul(pss[:, 0:nk * 128],
                              qrb[i % 2][0:128, h * 128:(h + 1) * 128],
                              kr_sb[0:128, g2 * 512:g2 * 512 + nk * 128],
                              start=False, stop=True))[-1],
                    wait=tuple(wt), sem="t")
                t_scores_last[(i, h)] = p.v("t")
                wt = [("t", p.v("t"))]
                if u - 4 in v_clamp_u:
                    wt.append(("v", v_clamp_u[u - 4]))
                p.c("scalar", lambda sc, gq=gq, nk=nk, pss=pss: sc.activation(
                    expb[gq][:, 0:nk * 128], pss[:, 0:nk * 128], AF.Exp,
                    scale=SCALE_QK), wait=tuple(wt), sem="s")
                wt = [("s", p.v("s"))]
                if u - 4 in x_tr_u:
                    wt.append(("x", x_tr_u[u - 4]))

                def clamp_fn(v, i=i, h=h, g2=g2, nk=nk, sb=gq, lo_j=lo_j, ng=ng, g_band=g_band):
                    v.tensor_scalar(expb[sb][:, 0:nk * 128], expb[sb][:, 0:nk * 128],
                                    CLAMP_HI, None, ALU.min)
                    if g2 != g_band:
                        return v.tensor_scalar(
                            prtm[sb][:, 0:nk * 128], expb[sb][:, 0:nk * 128],
                            CLAMP_LO, None, ALU.max, op1=ALU.add,
                            accum_out=rsb[:, h * 8 + g2:h * 8 + g2 + 1])
                    v.tensor_scalar(prtm[sb][:, 0:nk * 128], expb[sb][:, 0:nk * 128],
                                    CLAMP_LO, None, ALU.max)
                    last = None
                    nxt = ng
                    for t2 in range(nk):
                        s = 4 * g2 + t2
                        dst = prtm[sb][:, t2 * 128:(t2 + 1) * 128]
                        acc = rsb[:, h * 8 + g2:h * 8 + g2 + 1] if t2 == 0 \
                            else rsb[:, h * 8 + nxt:h * 8 + nxt + 1]
                        if t2 > 0:
                            nxt += 1
                        if s < lo_j:
                            last = v.tensor_scalar(dst, dst, 1.0, None, ALU.mult,
                                                   op1=ALU.add, accum_out=acc)
                        else:
                            k = 2 * i + (s - lo_j)
                            last = v.scalar_tensor_tensor(
                                dst, dst, 1.0, maskb[:, k * 128:(k + 1) * 128],
                                ALU.mult, ALU.mult, accum_out=acc)
                    return last
                p.c("vector", clamp_fn, wait=tuple(wt), sem="v")
                v_clamp_u[u] = p.v("v")

            def emit_back(u, i=i, ng=ng):
                h, g2, nk = units[u]
                gq = u % 4
                avb = ps[2 + h % 2]
                for t2 in range(nk):
                    wt = [("v", v_clamp_u[u])] if t2 == 0 else []
                    if t2 == 0 and u - 4 in t_av_u:
                        wt.append(("t", t_av_u[u - 4]))
                    p.dma("sync", lambda e, gq=gq, t2=t2: e.dma_start_transpose(
                        out=prT[gq][:, t2 * 128:(t2 + 1) * 128],
                        in_=prtm[gq][:, t2 * 128:(t2 + 1) * 128]),
                        wait=tuple(wt), sem="x")
                x_tr_u[u] = p.v("x")
                p.c("tensor", lambda tn, i=i, h=h, g2=g2, nk=nk, gq=gq, avb=avb: [
                    tn.matmul(avb[:, 0:128], prT[gq][:, t2 * 128:(t2 + 1) * 128],
                              v_sb[:, (4 * g2 + t2) * 2048 + h * 128:
                                   (4 * g2 + t2) * 2048 + h * 128 + 128],
                              start=(g2 == 0 and t2 == 0),
                              stop=(4 * g2 + t2 == PA[i] - 1))
                    for t2 in range(nk)][-1],
                    wait=(("x", x_tr_u[u]),), sem="t")
                t_av_u[u] = p.v("t")
                if g2 == ng - 1:
                    p.c("vector", lambda v, h=h, avb=avb: v.tensor_copy(
                        attn_tm[:, h * 128:(h + 1) * 128], avb[:, 0:128]),
                        wait=(("t", t_av_u[u]),), sem="v")
                    epi_v[(i, h)] = p.v("v")

            for k in range(len(units) + SKEW):
                if k >= SKEW:
                    emit_back(k - SKEW)
                if k < len(units):
                    emit_front(k)

            nkb = min(4, PA[i] - 4 * g_band)
            nrs = ((PA[i] + 3) // 4) + max(0, nkb - 1)

            def slot_epi(v, i=i, nrs=nrs):
                v.drain()
                v.tensor_reduce(
                    rsb[:, 128:144].rearrange("p (a b) -> p a b", b=1),
                    rsb[:, 0:128].rearrange("p (a b) -> p a b", b=8)[:, :, 0:nrs],
                    axis=AX.X, op=ALU.add)
                v.drain()
                v.reciprocal(rsb[:, 144:160], rsb[:, 128:144])
                v.drain()
                last = None
                for h in range(H):
                    last = v.tensor_scalar(attn_tm[:, h * 128:(h + 1) * 128],
                                           attn_tm[:, h * 128:(h + 1) * 128],
                                           rsb[:, 144 + h:145 + h], None, ALU.mult)
                return last
            p.c("vector", slot_epi, wait=(("v", p.v("v")),), sem="v")

            attn_v = p.v("v")
            for kb in range(4):
                wt = [("v", attn_v)] if kb < 2 else [("v", p.v("v"))]
                p.c("tensor", lambda tn, kb=kb: [
                    tn.transpose(psb[kb % 2][:, q * 128:q * 128 + 128],
                                 attn_tm[:, (kb * 4 + q) * 128:(kb * 4 + q) * 128 + 128],
                                 ident[:]) for q in range(4)][-1],
                    wait=tuple(wt), sem="t")
                p.c("vector", lambda v, kb=kb: [
                    v.tensor_copy(attn_fm[:, (kb * 4 + q) * 128:(kb * 4 + q) * 128 + 128],
                                  psb[kb % 2][:, q * 128:q * 128 + 128]) for q in range(4)][-1],
                    wait=(("t", p.v("t")),), sem="v")
            afm_v = p.v("v")
            for ncol in range(4):
                wt = [("v", afm_v)] if ncol == 0 else \
                     ([("v", p.v("v"))] if ncol >= 2 else [])
                p.c("tensor", lambda tn, ncol=ncol: [
                    tn.matmul(ps[2 + ncol % 2][:, 0:512],
                              attn_fm[:, afc * 128:(afc + 1) * 128],
                              wo[:, afc * 2048 + ncol * 512: afc * 2048 + ncol * 512 + 512],
                              start=(afc == 0), stop=(afc == 15)) for afc in range(16)][-1],
                    wait=tuple(wt), sem="t")
                p.c("vector", lambda v, ncol=ncol: v.tensor_copy(
                    opstg[:, ncol * 512:(ncol + 1) * 512], ps[2 + ncol % 2][:, 0:512]),
                    wait=(("t", p.v("t")),) + ((("q", p.v("q")),) if ncol == 0 else ()),
                    sem="v")
            p.dma("sync", lambda e, i=i: e.dma_start(out=op_scr[i], in_=opstg[:]),
                  wait=(("v", p.v("v")),), sem="q")
        b_w = p.v("w")
        b_q = p.v("q")
        tap_out(("op_scr",), (("q", b_q),))
        p.mark["Be"] = len(p.steps)
        _open.remove(bctx); bctx.close()
        _open.remove(abctx); abctx.close()
        if end_at == "B":
            raise _EndBuild
        cctx = ExitStack()
        _open.append(cctx)
        CC = cctx.enter_context
        xno2 = CC(nc.sbuf_tensor("xno2", [128, 16 * 1024], BF16))
        ffn = CC(nc.sbuf_tensor("ffn", [128, 8 * 2048], F32))
        h_sb = CC(nc.sbuf_tensor("h_sb", [128, 16 * 1024], BF16))
        sil = [CC(nc.sbuf_tensor(f"sil{i}", [128, 512], BF16)) for i in range(2)]
        w1s = [CC(nc.sbuf_tensor(f"w1s{i}", [128, 16 * 128], BF16)) for i in range(2)]
        wgs = [CC(nc.sbuf_tensor(f"wgs{i}", [128, 16 * 128], BF16)) for i in range(2)]
        w2s = [CC(nc.sbuf_tensor(f"w2s{i}", [128, 16 * 512], BF16)) for i in range(2)]
        gwn = CC(nc.sbuf_tensor("gwn", [128, 16 * NE], BF16))
        glog = CC(nc.sbuf_tensor("glog", [128, 2 * NE + 4], F32))
        grk = CC(nc.sbuf_tensor("grk", [128, 49 + NE], F32))
        w_all = CC(nc.sbuf_tensor("w_all", [128, 8 * NE], F32))
        opin = CC(nc.sbuf_tensor("opin", [128, D], F32))
        xre = CC(nc.sbuf_tensor("xre", [128, D], F32))
        ystg = CC(nc.sbuf_tensor("ystg", [128, D], F32))

        def mk_xno2_dma(i):
            def f(e):
                rv = j_rv(e, i)
                return e.dma_start(
                    out=xno2[:].rearrange("p (dc n) -> p dc n", n=1024)[:, :, i * 128:(i + 1) * 128],
                    in_=xn_scr[ds(rv, 1)].squeeze(0).rearrange("p (dc n) -> p dc n", dc=16))
            return f
        for i in range(NO):
            p.dma("sync", mk_xno2_dma(i), wait=((("q", b_q),) if i == 0 else ()), sem="q")
        p.dma("gpsimd", lambda e: e.dma_start(
            out=gwn[:].rearrange("p (dc n) -> p dc n", n=NE),
            in_=w["gate_w"].rearrange("(dc p) n -> p dc n", p=128)),
            wait=(("q", b_q),))
        xno2_q = p.v("q")
        gw_w = p.v("w")

        for tc in range(NO):
            p.c("tensor", lambda tn, tc=tc: [
                tn.matmul(ps[4][:, 0:NE],
                          xno2[:, dc * 1024 + tc * 128: dc * 1024 + tc * 128 + 128],
                          gwn[:, dc * NE:(dc + 1) * NE],
                          start=(dc == 0), stop=(dc == 15)) for dc in range(16)][-1],
                wait=(("q", xno2_q), ("w", gw_w)) if tc == 0 else (("v", p.v("v")),), sem="t")

            def gate_fn(v, tc=tc):
                v.tensor_add(glog[:, 0:NE], ps[4][:, 0:NE], gateb[:])
                v.drain()
                v.reduce_max(glog[:, 2 * NE:2 * NE + 1], glog[:, 0:NE], axis=AX.X)
                v.drain()
                return v.tensor_scalar_mul(glog[:, 2 * NE + 1:2 * NE + 2], glog[:, 2 * NE:2 * NE + 1], -1.0)
            p.c("vector", gate_fn, wait=(("t", p.v("t")),), sem="v")
            p.c("scalar", lambda sc: sc.activation(
                glog[:, NE:2 * NE], glog[:, 0:NE], AF.Exp,
                bias=glog[:, 2 * NE + 1:2 * NE + 2],
                accum_out=glog[:, 2 * NE + 2:2 * NE + 3]),
                wait=(("v", p.v("v")),), sem="s")

            def gate2_fn(v, tc=tc):
                v.reciprocal(glog[:, 2 * NE + 3:2 * NE + 4], glog[:, 2 * NE + 2:2 * NE + 3])
                v.drain()
                v.tensor_scalar(glog[:, NE:2 * NE], glog[:, NE:2 * NE],
                                glog[:, 2 * NE + 3:2 * NE + 4], None, ALU.mult)
                v.drain()
                v.tensor_add(glog[:, 0:NE], glog[:, NE:2 * NE], biase[:])
                v.drain()
                a3 = glog[:, 0:NE].unsqueeze(1).broadcast_to([128, NE, NE])
                b3 = glog[:, 0:NE].unsqueeze(2).broadcast_to([128, NE, NE])
                v.tensor_tensor(grk[:, 0:49].rearrange("p (i j) -> p i j", i=NE), a3, b3, ALU.is_gt)
                v.drain()
                v.tensor_reduce(grk[:, 49:49 + NE],
                                grk[:, 0:49].rearrange("p (i j) -> p i j", i=NE),
                                axis=AX.X, op=ALU.add)
                v.drain()
                v.tensor_single_scalar(grk[:, 49:49 + NE], grk[:, 49:49 + NE], 2.5, ALU.is_lt)
                v.drain()
                v.tensor_mul(grk[:, 0:NE], glog[:, NE:2 * NE], grk[:, 49:49 + NE])
                v.drain()
                v.reduce_sum(glog[:, 2 * NE:2 * NE + 1], grk[:, 0:NE], axis=AX.X)
                v.drain()
                v.reciprocal(glog[:, 2 * NE + 1:2 * NE + 2], glog[:, 2 * NE:2 * NE + 1])
                v.drain()
                return v.tensor_scalar(w_all[:, tc * NE:(tc + 1) * NE], grk[:, 0:NE],
                                       glog[:, 2 * NE + 1:2 * NE + 2], None, ALU.mult)
            p.c("vector", gate2_fn, wait=(("s", p.v("s")),), sem="v")
        gate_v = p.v("v")

        up_dma_w = {}
        t_up_last = {}
        t_dn_last = {}
        v_mul_u = {}
        v_dn_u = {}
        uu_c = [0]
        dd_c = [0]
        for ei in range(8):
            shared = (ei == 0)
            e_idx = ei - 1
            w1h = w["Ws1"] if shared else w["W1"][e_idx]
            wgh = w["Wsg"] if shared else w["Wg"][e_idx]
            w2h = w["Ws2"] if shared else w["W2"][e_idx]
            for hc in range(16):
                bi = hc % 2
                wt = []
                if (ei, hc - 2) in t_up_last:
                    wt.append(("t", t_up_last[(ei, hc - 2)]))
                elif hc < 2 and (ei - 1, 14 + hc) in t_up_last:
                    wt.append(("t", t_up_last[(ei - 1, 14 + hc)]))
                p.dma("gpsimd", lambda e, w1h=w1h, hc=hc, bi=bi: e.dma_start(
                    out=w1s[bi][:].rearrange("p (b n) -> p b n", b=16),
                    in_=w1h[:, hc * 128:(hc + 1) * 128].rearrange("(b p) n -> p b n", p=128)),
                    wait=tuple(wt), sem="w")
                p.dma("gpsimd", lambda e, wgh=wgh, hc=hc, bi=bi: e.dma_start(
                    out=wgs[bi][:].rearrange("p (b n) -> p b n", b=16),
                    in_=wgh[:, hc * 128:(hc + 1) * 128].rearrange("(b p) n -> p b n", p=128)),
                    sem="w")
                up_w = p.v("w")
                for t2 in range(2):
                    uu = uu_c[0]
                    uu_c[0] += 1
                    pa, pb = (ps[0], ps[1]) if uu % 2 == 0 else (ps[2], ps[3])
                    wt = [("w", up_w)]
                    if ei == 0 and hc == 0 and t2 == 0:
                        wt.append(("v", gate_v))
                    if uu - 2 in v_mul_u:
                        wt.append(("v", v_mul_u[uu - 2]))
                    p.c("tensor", lambda tn, hc=hc, t2=t2, bi=bi, pa=pa, pb=pb: [
                        (tn.matmul(pa[:, 0:512],
                                   w1s[bi][:, dc * 128:(dc + 1) * 128],
                                   xno2[:, dc * 1024 + t2 * 512: dc * 1024 + t2 * 512 + 512],
                                   start=(dc == 0), stop=(dc == 15)),
                         tn.matmul(pb[:, 0:512],
                                   wgs[bi][:, dc * 128:(dc + 1) * 128],
                                   xno2[:, dc * 1024 + t2 * 512: dc * 1024 + t2 * 512 + 512],
                                   start=(dc == 0), stop=(dc == 15)))[-1] for dc in range(16)][-1],
                        wait=tuple(wt), sem="t")
                    t_up_last[(ei, hc)] = p.v("t")
                    wt = [("t", p.v("t"))]
                    if uu - 2 in v_mul_u:
                        wt.append(("v", v_mul_u[uu - 2]))
                    p.c("scalar", lambda sc, uu=uu, pa=pa: sc.activation(
                        sil[uu % 2][:], pa[:, 0:512], AF.Silu),
                        wait=tuple(wt), sem="s")
                    p.c("vector", lambda v, hc=hc, t2=t2, uu=uu, pb=pb: v.tensor_mul(
                        h_sb[:, hc * 1024 + t2 * 512: hc * 1024 + t2 * 512 + 512],
                        pb[:, 0:512], sil[uu % 2][:]),
                        wait=(("s", p.v("s")),), sem="v")
                    v_mul_u[uu] = p.v("v")
            up_v = p.v("v")
            for dcol in range(4):
                bi = dcol % 2
                wt = []
                if (ei, dcol - 2) in t_dn_last:
                    wt.append(("t", t_dn_last[(ei, dcol - 2)]))
                elif dcol < 2 and (ei - 1, 2 + dcol) in t_dn_last:
                    wt.append(("t", t_dn_last[(ei - 1, 2 + dcol)]))
                p.dma("gpsimd", lambda e, w2h=w2h, dcol=dcol, bi=bi: e.dma_start(
                    out=w2s[bi][:].rearrange("p (b n) -> p b n", b=16),
                    in_=w2h[:, dcol * 512:(dcol + 1) * 512].rearrange("(b p) n -> p b n", p=128)),
                    wait=tuple(wt), sem="w")
                dn_w = p.v("w")
                for tc in range(NO):
                    dd = dd_c[0]
                    dd_c[0] += 1
                    pdn = ps[4 + dd % 2]
                    wt = [("w", dn_w)]
                    if dcol == 0 and tc == 0:
                        wt.append(("v", up_v))
                    if dd - 2 in v_dn_u:
                        wt.append(("v", v_dn_u[dd - 2]))
                    p.c("tensor", lambda tn, dcol=dcol, tc=tc, bi=bi, pdn=pdn: [
                        tn.matmul(pdn[:, 0:512],
                                  h_sb[:, hc * 1024 + tc * 128: hc * 1024 + tc * 128 + 128],
                                  w2s[bi][:, hc * 512:(hc + 1) * 512],
                                  start=(hc == 0), stop=(hc == 15)) for hc in range(16)][-1],
                        wait=tuple(wt), sem="t")
                    t_dn_last[(ei, dcol)] = p.v("t")
                    if shared:
                        p.c("vector", lambda v, dcol=dcol, tc=tc, pdn=pdn: v.tensor_copy(
                            ffn[:, tc * 2048 + dcol * 512: tc * 2048 + dcol * 512 + 512],
                            pdn[:, 0:512]),
                            wait=(("t", p.v("t")),), sem="v")
                    else:
                        p.c("vector", lambda v, dcol=dcol, tc=tc, pdn=pdn, e_idx=e_idx: v.scalar_tensor_tensor(
                            ffn[:, tc * 2048 + dcol * 512: tc * 2048 + dcol * 512 + 512],
                            pdn[:, 0:512], w_all[:, tc * NE + e_idx: tc * NE + e_idx + 1],
                            ffn[:, tc * 2048 + dcol * 512: tc * 2048 + dcol * 512 + 512],
                            ALU.mult, ALU.add),
                            wait=(("t", p.v("t")),), sem="v")
                    v_dn_u[dd] = p.v("v")
        ffn_v = p.v("v")

        def mk_xre_c(i):
            def f(e):
                rv = jx_rv(e, i)
                return e.dma_start(out=xre[:], in_=x_h[ds(rv, 128), :])
            return f
        for tc in range(NO):
            p.dma("sync", lambda e, tc=tc: e.dma_start(out=opin[:], in_=op_scr[tc]),
                  wait=((("v", ffn_v),) if tc == 0 else (("o", p.v("o")),)), sem="q")
            p.dma("sync", mk_xre_c(tc), sem="x")
            p.c("vector", lambda v, tc=tc: v.tensor_add(
                ystg[:], ffn[:, tc * 2048:(tc + 1) * 2048], opin[:]),
                wait=(("q", p.v("q")), ("o", p.v("o"))), sem="v")
            p.c("vector", lambda v: v.tensor_add(ystg[:], ystg[:], xre[:]),
                wait=(("x", p.v("x")),), sem="v")
            p.dma("sync", lambda e, tc=tc: e.dma_start(out=y_h[tc], in_=ystg[:]),
                  wait=(("v", p.v("v")),), sem="o")
        _open.remove(cctx); cctx.close()
        p.step("sync", None, wait=(("o", p.v("o")),))
      except _EndBuild:
        for _st in reversed(_open):
            _st.close()
      if True:
        for sname in ("x", "w", "w2", "t", "v", "s", "q", "o"):
            if p.v(sname):
                p.step("sync", None, wait=((sname, p.v(sname)),))

        mk = getattr(p, "mark", {})
        a1e = mk.get("A1e", 10**9); a2e = mk.get("A2e", 10**9); be = mk.get("Be", 10**9)

        def phase_of(idx):
            if idx < a1e:
                return "phA"
            if idx < a2e:
                return "phA2"
            if idx < be:
                return "phB"
            return "phC"

        @block.sync
        def _(e):
            p.emit("sync", e, sems, marks=phase_of)

        @block.gpsimd
        def _(e):
            p.emit("gpsimd", e, sems, marks=phase_of)

        @block.tensor
        def _(e):
            p.emit("tensor", e, sems, marks=phase_of)

        @block.vector
        def _(e):
            p.emit("vector", e, sems, marks=phase_of)

        @block.scalar
        def _(e):
            p.emit("scalar", e, sems, marks=phase_of)

    return nc


def make_consts():
    ident = np.eye(128, dtype=ml_dtypes.bfloat16)
    ones = np.ones((128, 128), dtype=np.float32)
    causal = np.tril(np.ones((128, 128), np.float32))
    zeros = np.zeros((128, 128), dtype=np.float32)
    masks = np.concatenate([ones, zeros], axis=1)
    pos = (np.arange(NT)[None, :] * 128 + np.arange(128)[:, None]).astype(np.float32)
    return ident, masks, pos, (ones, causal, zeros)


def make_mask_band(own):
    ident, masks, pos, (ones, causal, zeros) = make_consts()
    band = np.zeros((128, 16 * 128), np.float32)
    for i, j in enumerate(own):
        lo = min(OWN_EVEN[i], OWN_ODD[i])
        for b in range(2):
            s = lo + b
            tile = causal if s == j else (ones if s < j else zeros)
            band[:, (2 * i + b) * 128:(2 * i + b + 1) * 128] = tile
    return band


def make_trig_tables():
    import jax
    import jax.numpy as jnp
    with jax.default_device(jax.devices("cpu")[0]):
        inv_freq = 10000.0 ** (-jnp.arange(0, 2 * 32, 2, dtype=jnp.float32) / (2 * 32))
        ang = jnp.arange(S).astype(jnp.float32)[:, None] * inv_freq[None, :]
        cos, sin = jnp.cos(ang), jnp.sin(ang)
    def lay(a):
        return np.ascontiguousarray(
            np.asarray(a, np.float32).reshape(16, 128, 32).transpose(1, 2, 0).reshape(128, 512))
    return lay(sin), lay(cos)


def kernel(**inputs):
    x = np.asarray(inputs["L0_attn_x"], np.float32)
    ident, masks, pos, _ = make_consts()
    base = {
        "W_DQ": inputs["L0_attn_W_DQ"], "W_UQ": inputs["L0_attn_W_UQ"],
        "W_QR": inputs["L0_attn_W_QR"], "W_DKV": inputs["L0_attn_W_DKV"],
        "W_UK": inputs["L0_attn_W_UK"], "W_UV": inputs["L0_attn_W_UV"],
        "W_KR": inputs["L0_attn_W_KR"], "W_O": inputs["L0_attn_W_O"],
        "W1": inputs["L0_ffn_W1"], "Wg": inputs["L0_ffn_Wg"], "W2": inputs["L0_ffn_W2"],
        "Ws1": inputs["L0_ffn_Ws1"], "Wsg": inputs["L0_ffn_Wsg"], "Ws2": inputs["L0_ffn_Ws2"],
        "gate_w": inputs["L0_ffn_gate_w"],
        "gate_b_rep": np.ascontiguousarray(np.tile(np.asarray(inputs["L0_ffn_gate_b"], np.float32)[None, :], (128, 1))),
        "biases_rep": np.ascontiguousarray(np.tile(np.asarray(inputs["L0_ffn_biases"], np.float32)[None, :], (128, 1))),
        "ident": ident, "pos": pos,
    }
    base = {k: np.ascontiguousarray(np.asarray(v, dtype=(ml_dtypes.bfloat16 if k == "ident" else np.float32)))
            for k, v in base.items()}
    nc = build_nc()
    sin_tab, cos_tab = make_trig_tables()
    in_maps = []
    for c in range(8):
        b, par = c // 2, c % 2
        own = OWN_EVEN if par == 0 else OWN_ODD
        m = dict(base)
        m["x"] = np.ascontiguousarray(x[b])
        m["own_j"] = np.asarray([own], np.int32)
        m["mask_band"] = make_mask_band(own).astype(ml_dtypes.bfloat16)
        m["zpad"] = np.zeros((64, 16 * 128), ml_dtypes.bfloat16)
        m["sin_t"] = sin_tab
        m["cos_t"] = cos_tab
        in_maps.append(m)
    res = run_bass_kernel_spmd(nc, in_maps, core_ids=list(range(8)))
    out = np.zeros((B, S, D), np.float32)
    for c in range(8):
        b, par = c // 2, c % 2
        own = OWN_EVEN if par == 0 else OWN_ODD
        yc = res.results[c]["y"]
        for i, j in enumerate(own):
            out[b, j * 128:(j + 1) * 128, :] = yc[i]
    return out


def _mask_sel(own):
    sel = []
    for i, j in enumerate(own):
        for s in range(PA[i]):
            sel.append(0 if s < j else (128 if s == j else 256))
    return sel


# version 14 (final)
# speedup vs baseline: 1.0961x; 1.0961x over previous
import math
from contextlib import ExitStack

import numpy as np
import ml_dtypes

import concourse.bass as bass
import concourse.mybir as mybir
from concourse.bass_utils import run_bass_kernel_spmd
from concourse.expressions import RuntimeValue

F32 = mybir.dt.float32
BF16 = mybir.dt.bfloat16
I32 = mybir.dt.int32
AF = mybir.ActivationFunctionType
ALU = mybir.AluOpType
AX = mybir.AxisListType
ds = bass.ds

B, S, D = 4, 2048, 2048
H, D_H, D_HR, D_LAT = 16, 128, 64, 512
NT = S // 128
NO = 8
NE = 7
D_QK = D_H + D_HR
SQRT_DQK_F32 = float(np.float32(np.sqrt(np.float32(float(D_QK)))))
SCALE_QK = 1.0 / SQRT_DQK_F32
CLIP = 80.0
MASK_NEG = -100000.0
LN_EPS = 1e-5
CLAMP_HI = math.exp(CLIP / SQRT_DQK_F32)
CLAMP_LO = math.exp(-CLIP / SQRT_DQK_F32)

OWN_EVEN = [0, 15, 3, 12, 4, 11, 7, 8]
OWN_ODD = [1, 14, 2, 13, 5, 10, 6, 9]
PA = [max(a, b) + 1 for a, b in zip(OWN_EVEN, OWN_ODD)]
NSLOT = sum(PA)

SIN_C = [0.9999999999999537, -0.1666666666647968, 0.008333333320897147,
         -0.00019841266760649062, 2.7556957552677794e-06,
         -2.5030378131205352e-08, 1.5411754176956953e-10]
TWO_PI = 2.0 * math.pi
PI = math.pi
INVF = [10000.0 ** (-k / 32.0) for k in range(32)]
C2PI_1 = 6.283203125
C2PI_2 = -1.7814338207244873e-05
C2PI_3 = -3.482206523131026e-09

ENGINES = ("sync", "gpsimd", "tensor", "vector", "scalar")


class _EndBuild(Exception):
    pass


class Prog:

    def __init__(self):
        self.steps = []
        self.n = {}
        self.mark = {}

    def v(self, s):
        return self.n.get(s, 0)

    def step(self, eng, fn=None, wait=(), inc=()):
        for s, a in inc:
            self.n[s] = self.n.get(s, 0) + a
        self.steps.append((eng, fn, tuple(wait), tuple(inc)))

    def dma(self, eng, fn, wait=(), sem="w"):
        self.step(eng, fn, wait, ((sem, 16),))
        return self.v(sem)

    def c(self, eng, fn, wait=(), sem=None):
        inc = ((sem, 1),) if sem else ()
        self.step(eng, fn, wait, inc)
        return self.v(sem) if sem else None

    def emit(self, name, eng, sems, marks=None):
        nc_b = getattr(eng, "bass", None)
        cur = None
        for idx, (e, fn, wait, inc) in enumerate(self.steps):
            if marks is not None:
                ph = marks(idx)
                if ph != cur and nc_b is not None:
                    if cur is not None:
                        nc_b.pop_named_scope(cur)
                    if ph is not None:
                        nc_b.push_named_scope(ph)
                    cur = ph
            if e != name:
                continue
            for s, v in wait:
                eng.wait_ge(sems[s], v)
            ins = fn(eng) if fn is not None else None
            if inc:
                assert ins is not None and hasattr(ins, "then_inc"), (fn, inc)
                assert len(inc) == 1
                (s0, a0) = inc[0]
                ins.then_inc(sems[s0], a0)
        if marks is not None and cur is not None and nc_b is not None:
            nc_b.pop_named_scope(cur)


def build_nc(debug_taps=(), end_at="full", a1_stages=("xn", "ckv", "kr", "kc", "v"), a1_groups=4, pre_stages=("trig", "wtrans")):
    nc = bass.Bass("TRN2", target_bir_lowering=False, detect_race_conditions=False)

    x_h = nc.dram_tensor("x", [S, D], F32, kind="ExternalInput").ap()
    w = {}
    for name, shape in [
        ("W_DQ", [D_LAT, D]), ("W_UQ", [H * D_H, D_LAT]), ("W_QR", [H * D_HR, D_LAT]),
        ("W_DKV", [D_LAT, D]), ("W_UK", [H * D_H, D_LAT]), ("W_UV", [H * D_H, D_LAT]),
        ("W_KR", [D_HR, D]), ("W_O", [D, H * D_H]),
        ("W1", [NE, D, D]), ("Wg", [NE, D, D]), ("W2", [NE, D, D]),
        ("Ws1", [D, D]), ("Wsg", [D, D]), ("Ws2", [D, D]), ("gate_w", [D, NE]),
    ]:
        w[name] = nc.dram_tensor(name, shape, F32, kind="ExternalInput").ap()
    gate_b_h = nc.dram_tensor("gate_b_rep", [128, NE], F32, kind="ExternalInput").ap()
    biases_h = nc.dram_tensor("biases_rep", [128, NE], F32, kind="ExternalInput").ap()
    ident_h = nc.dram_tensor("ident", [128, 128], BF16, kind="ExternalInput").ap()
    maskb_h = nc.dram_tensor("mask_band", [128, 16 * 128], BF16, kind="ExternalInput").ap()
    zpad_h = nc.dram_tensor("zpad", [64, 16 * 128], BF16, kind="ExternalInput").ap()
    sin_th = nc.dram_tensor("sin_t", [128, 512], F32, kind="ExternalInput").ap()
    cos_th = nc.dram_tensor("cos_t", [128, 512], F32, kind="ExternalInput").ap()
    pos_h = nc.dram_tensor("pos", [128, NT], F32, kind="ExternalInput").ap()
    ownj_h = nc.dram_tensor("own_j", [1, NO], I32, kind="ExternalInput").ap()

    y_h = nc.dram_tensor("y", [NO, 128, D], F32, kind="ExternalOutput").ap()

    xn_scr = nc.dram_tensor("xn_scr", [NT, 128, D], BF16, kind="Internal").ap()
    kc_scr = nc.dram_tensor("kc_scr", [H, NT, 128, 128], BF16, kind="Internal").ap()
    qc_scr = nc.dram_tensor("qc_scr", [NO, H, 128, 128], BF16, kind="Internal").ap()
    qr_scr = nc.dram_tensor("qr_scr", [NO, H, 64, 128], BF16, kind="Internal").ap()
    op_scr = nc.dram_tensor("op_scr", [NO, 128, D], F32, kind="Internal").ap()
    scr = {"xn_scr": xn_scr, "kc_scr": kc_scr, "qc_scr": qc_scr,
           "qr_scr": qr_scr, "op_scr": op_scr}
    tap_shapes = {"xn_scr": ([NT, 128, D], BF16), "kc_scr": ([H, NT, 128, 128], BF16),
                  "qc_scr": ([NO, H, 128, 128], BF16), "qr_scr": ([NO, H, 64, 128], BF16),
                  "op_scr": ([NO, 128, D], F32),
                  "v_sb": ([128, NT * 2048], BF16), "kr_sb": ([128, S], BF16),
                  "sin_sb": ([128, 512], F32), "cos_sb": ([128, 512], F32)}
    taps = {t: nc.dram_tensor("tap_" + t, tap_shapes[t][0], tap_shapes[t][1],
                              kind="ExternalOutput").ap() for t in debug_taps}

    p = Prog()
    ctx = ExitStack()
    with ctx:
      _open = []
      try:
        E = ctx.enter_context

        ident = E(nc.sbuf_tensor("ident_sb", [128, 128], BF16))
        pos_sb = E(nc.sbuf_tensor("pos_sb", [128, NT], F32))
        ownj = E(nc.sbuf_tensor("ownj_sb", [1, NO], I32))
        gateb = E(nc.sbuf_tensor("gateb_sb", [128, NE], F32))
        biase = E(nc.sbuf_tensor("biase_sb", [128, NE], F32))
        abctx = ExitStack()
        _open.append(abctx)
        AB = abctx.enter_context
        sin_sb = AB(nc.sbuf_tensor("sin_sb", [128, 512], F32))
        cos_sb = AB(nc.sbuf_tensor("cos_sb", [128, 512], F32))
        v_sb = AB(nc.sbuf_tensor("v_sb", [128, NT * 2048], BF16))
        kr_sb = AB(nc.sbuf_tensor("kr_sb", [128, S], BF16))

        tr1 = AB(nc.sbuf_tensor("tr1", [128, 512], F32))
        tr2 = AB(nc.sbuf_tensor("tr2", [128, 512], F32))

        ps = [E(nc.psum_tensor(f"ps{i}", [128, 512], F32)) for i in range(6)]
        psb = [E(nc.psum_tensor(f"psb{i}", [128, 512], BF16)) for i in range(2)]

        sems = {n: E(nc.semaphore("sem_" + n)) for n in
                ["x", "w", "t", "v", "s", "q", "o", "w2"]}

        block = E(nc.Block())
        _regs = {}

        def j_rv(e, i):
            key = (e.engine, "j", i)
            if key not in _regs:
                r = e.alloc_register(f"j{i}")
                e.reg_load(r, ownj[0:1, i:i + 1])
                _regs[key] = r
            return RuntimeValue(_regs[key], 0, 15)

        def jx_rv(e, i):
            key = (e.engine, "jx", i)
            if key not in _regs:
                r = e.alloc_register(f"jx{i}")
                e.reg_load(r, ownj[0:1, i:i + 1])
                e.reg_mul(r, r, 128)
                _regs[key] = r
            return RuntimeValue(_regs[key], 0, 1920)


        for dst, src in [(sin_sb, sin_th), (cos_sb, cos_th), (ident, ident_h),
                         (pos_sb, pos_h), (ownj, ownj_h), (gateb, gate_b_h),
                         (biase, biases_h)]:
            p.dma("sync", lambda e, d=dst, s=src: e.dma_start(out=d[:], in_=s[:]), sem="q")
        consts_q = p.v("q")

        trig_v = 0

        actx = ExitStack()
        _open.append(actx)
        AC = actx.enter_context
        wdkv = AC(nc.sbuf_tensor("wdkv", [128, 16 * 512], BF16))
        wuk = AC(nc.sbuf_tensor("wuk", [128, 4 * 2048], BF16))
        wuv = AC(nc.sbuf_tensor("wuv", [128, 4 * 2048], BF16))
        wkr = AC(nc.sbuf_tensor("wkr", [128, 16 * 64], BF16))
        wnat = AC(nc.sbuf_tensor("wnat", [128, 8192], BF16))

        def wtrans(nat_h, nrows, ncols, dst_sb, wnat):
            rb = max(1, nrows // 128)
            prows = min(nrows, 128)
            cb = ncols // 128
            piece = max(1, min(rb, 8192 // ncols))
            for b0 in range(0, rb, piece):
                bn = min(piece, rb - b0)
                nat_ap = nat_h[b0 * 128:(b0 * 128 + bn * prows), :]
                if nrows >= 128:
                    nat_ap = nat_ap.rearrange("(b q) n -> q b n", q=128)
                else:
                    nat_ap = nat_ap.unsqueeze(1)
                wq = p.dma("gpsimd", lambda e, a=nat_ap, bn=bn: e.dma_start(
                    out=wnat[0:prows, 0:bn * ncols].rearrange("q (b n) -> q b n", b=bn), in_=a),
                    wait=(("t", p.v("t")),))
                tiles = [(a, b1) for a in range(cb) for b1 in range(bn)]
                v_base = p.v("v")
                for nb0 in range(0, len(tiles), 4):
                    batch = tiles[nb0:nb0 + 4]
                    bank = (nb0 // 4) % 2
                    if nb0 == 0:
                        wt = (("w", wq), ("v", v_base))
                    elif nb0 >= 8:
                        wt = (("v", p.v("v") - 1),)
                    else:
                        wt = ()
                    p.c("tensor", lambda t, batch=batch, bank=bank: [t.transpose(
                        psb[bank][0:128, k * 128:k * 128 + prows],
                        wnat[0:prows, b1 * ncols + a * 128: b1 * ncols + a * 128 + 128],
                        ident[0:prows, 0:prows]) for k, (a, b1) in enumerate(batch)][-1],
                        wait=wt, sem="t")
                    if prows == 128 and all(
                            (batch[k][0] == batch[0][0] and batch[k][1] == batch[0][1] + k)
                            for k in range(len(batch))):
                        a0, b10 = batch[0]
                        p.c("vector", lambda v, a0=a0, b10=b10, b0=b0, bank=bank, nkb=len(batch): v.tensor_copy(
                            dst_sb[:, a0 * nrows + (b0 + b10) * 128:
                                   a0 * nrows + (b0 + b10) * 128 + nkb * 128],
                            psb[bank][0:128, 0:nkb * 128]),
                            wait=(("t", p.v("t")),), sem="v")
                    else:
                        p.c("vector", lambda v, batch=batch, bank=bank, b0=b0: [v.tensor_copy(
                            dst_sb[:, a * nrows + (b0 + b1) * 128: a * nrows + (b0 + b1) * 128 + prows],
                            psb[bank][0:128, k * 128:k * 128 + prows]) for k, (a, b1) in enumerate(batch)][-1],
                            wait=(("t", p.v("t")),), sem="v")
            return p.v("v")

        p.step("gpsimd", None, wait=(("q", consts_q),))
        p.step("tensor", None, wait=(("q", consts_q),))
        if "wtrans" in pre_stages:
            wtrans(w["W_DKV"], D_LAT, D, wdkv, wnat)
            wtrans(w["W_UK"], H * D_H, D_LAT, wuk, wnat)
            wtrans(w["W_UV"], H * D_H, D_LAT, wuv, wnat)
            kvw_v = wtrans(w["W_KR"], D_HR, D, wkr, wnat)
        else:
            kvw_v = p.v("v")

        p.dma("sync", lambda e: e.dma_start(out=kr_sb[64:128, :], in_=zpad_h[:]), sem="q")
        x_bufs = [AC(nc.sbuf_tensor(f"xb{i}", [128, D], F32)) for i in range(2)]
        xn_tm = AC(nc.sbuf_tensor("xn_tm", [128, D], BF16))
        xc_sb = AC(nc.sbuf_tensor("xc_sb", [128, D], F32))
        sq_sb = AC(nc.sbuf_tensor("sq_sb", [128, D], F32))
        ln_m = AC(nc.sbuf_tensor("ln_m", [128, 8], F32))
        xn_fm = AC(nc.sbuf_tensor("xn_fm", [128, 16 * 512], BF16))
        ckv = AC(nc.sbuf_tensor("ckv", [128, 4 * 512], BF16))
        ev0 = AC(nc.sbuf_tensor("ev0", [128, 512], BF16))
        ev1 = AC(nc.sbuf_tensor("ev1", [128, 512], BF16))
        krst = AC(nc.sbuf_tensor("krst", [128, 64], BF16))

        for t in range(2):
            p.dma("sync", lambda e, t=t: e.dma_start(out=x_bufs[t][:], in_=x_h[t * 128:(t + 1) * 128, :]),
                  sem="x")

        kc_dma_w = []
        xn_dma_w = {}

        for g in range(a1_groups):
            for ti in range(4):
                t = g * 4 + ti
                xb = x_bufs[t % 2]
                p.c("vector", lambda v, xb=xb: (
                    v.reduce_sum(ln_m[:, 0:1], xb[:], axis=AX.X),
                    v.drain(),
                    v.tensor_scalar_mul(ln_m[:, 1:2], ln_m[:, 0:1], 1.0 / D),
                    v.drain(),
                    v.tensor_scalar(xc_sb[:], xb[:], ln_m[:, 1:2], None, ALU.subtract))[-1],
                    wait=(("x", 16 * (t + 1)),), sem="v")
                p.c("scalar", lambda sc: sc.activation(
                    sq_sb[:], xc_sb[:], AF.Square, accum_out=ln_m[:, 2:3]),
                    wait=(("v", p.v("v")),), sem="s")
                p.c("vector", lambda v: v.tensor_scalar(
                    ln_m[:, 3:4], ln_m[:, 2:3], 1.0 / D, LN_EPS, ALU.mult, ALU.add),
                    wait=(("s", p.v("s")),), sem="v")
                p.c("scalar", lambda sc: sc.activation(ln_m[:, 4:5], ln_m[:, 3:4], AF.Sqrt),
                    wait=(("v", p.v("v")),), sem="s")
                p.c("vector", lambda v: (
                    v.reciprocal(ln_m[:, 5:6], ln_m[:, 4:5]),
                    v.drain(),
                    v.tensor_scalar(xn_tm[:], xc_sb[:], ln_m[:, 5:6], None, ALU.mult))[-1],
                    wait=(("s", p.v("s")), ("t", p.v("t"))), sem="v")
                xn_tm_v = p.v("v")
                if t + 2 < NT:
                    p.dma("sync", lambda e, t=t: e.dma_start(
                        out=x_bufs[t % 2][:], in_=x_h[(t + 2) * 128:(t + 3) * 128, :]),
                        wait=(("v", xn_tm_v),), sem="x")
                for kbatch in range(4):
                    wt = [("v", xn_tm_v)]
                    if kbatch == 0 and (t - 4) in xn_dma_w:
                        wt.append(("w", xn_dma_w[t - 4]))
                    p.c("tensor", lambda tn, kb=kbatch: [
                        tn.transpose(psb[kb % 2][:, q * 128:q * 128 + 128],
                                     xn_tm[:, (kb * 4 + q) * 128:(kb * 4 + q) * 128 + 128],
                                     ident[:]) for q in range(4)][-1],
                        wait=tuple(wt) + ((("v", p.v("v")),) if kbatch >= 2 else ()),
                        sem="t")
                    p.c("vector", lambda v, kb=kbatch, ti=ti: [
                        v.tensor_copy(xn_fm[:, (kb * 4 + q) * 512 + ti * 128:(kb * 4 + q) * 512 + ti * 128 + 128],
                                      psb[kb % 2][:, q * 128:q * 128 + 128]) for q in range(4)][-1],
                        wait=(("t", p.v("t")),), sem="v")
                xn_dma_w[t] = p.dma("gpsimd", lambda e, t=t, ti=ti: e.dma_start(
                    out=xn_scr[t].rearrange("q (dc n) -> q dc n", dc=16),
                    in_=xn_fm[:].rearrange("q (dc n) -> q dc n", dc=16)[:, :, ti * 128:ti * 128 + 128]),
                    wait=(("v", p.v("v")),))

            xn_group_v = p.v("v")
            if "ckv" not in a1_stages:
                continue
            for lc in range(4):
                wt = [("v", xn_group_v), ("w", p.v("w"))] if lc == 0 else \
                     ([("v", p.v("v"))] if lc >= 2 else [])
                p.c("tensor", lambda tn, lc=lc: [
                    tn.matmul(ps[lc % 2][:, 0:512],
                              wdkv[:, dc * 512 + lc * 128: dc * 512 + lc * 128 + 128],
                              xn_fm[:, dc * 512: dc * 512 + 512],
                              start=(dc == 0), stop=(dc == 15)) for dc in range(16)][-1],
                    wait=tuple(wt), sem="t")
                p.c("vector", lambda v, lc=lc: v.tensor_copy(
                    ckv[:, lc * 512:(lc + 1) * 512], ps[lc % 2][:, 0:512]),
                    wait=(("t", p.v("t")),), sem="v")
            ckv_v = p.v("v")
            if "kr" not in a1_stages:
                continue
            for ti in range(4):
                t = g * 4 + ti
                p.c("tensor", lambda tn, ti=ti: [
                    tn.matmul(ps[2][0:128, 0:64],
                              xn_fm[:, dc * 512 + ti * 128: dc * 512 + ti * 128 + 128],
                              wkr[:, dc * 64:(dc + 1) * 64],
                              start=(dc == 0), stop=(dc == 15)) for dc in range(16)][-1],
                    wait=((("v", p.v("v")),) if ti > 0 else (("v", ckv_v),)), sem="t")

                def rope_kr(v, t=t):
                    x1 = ps[2][:, 0:64:2]
                    x2 = ps[2][:, 1:64:2]
                    cosn = cos_sb[:, t:512:16]
                    sinn = sin_sb[:, t:512:16]
                    a = tr1[:, 0:32]
                    b = tr2[:, 0:32]
                    v.tensor_mul(a, x1, cosn)
                    v.tensor_mul(b, x2, sinn)
                    v.drain()
                    v.tensor_sub(krst[:, 0:64:2], a, b)
                    v.tensor_mul(a, x1, sinn)
                    v.tensor_mul(b, x2, cosn)
                    v.drain()
                    return v.tensor_add(krst[:, 1:64:2], a, b)
                p.c("vector", rope_kr, wait=(("t", p.v("t")),), sem="v")
                p.c("tensor", lambda tn: tn.transpose(psb[0][0:64, 0:128], krst[:], ident[:]),
                    wait=(("v", p.v("v")),), sem="t")
                p.c("vector", lambda v, t=t: v.tensor_copy(
                    kr_sb[0:64, t * 128:(t + 1) * 128], psb[0][0:64, 0:128]),
                    wait=(("t", p.v("t")),), sem="v")

            if "kc" not in a1_stages:
                continue
            for h in range(16):
                wt = [("v", p.v("v"))] if h < 2 else [("v", p.v("v") - 1)]
                p.c("tensor", lambda tn, h=h: [
                    tn.matmul(ps[2 + h % 2][:, 0:512],
                              wuk[:, lc * 2048 + h * 128: lc * 2048 + h * 128 + 128],
                              ckv[:, lc * 512: lc * 512 + 512],
                              start=(lc == 0), stop=(lc == 3)) for lc in range(4)][-1],
                    wait=tuple(wt), sem="t")
                evw = [("t", p.v("t"))]
                if len(kc_dma_w) >= 2:
                    evw.append(("w", kc_dma_w[-2]))
                p.c("vector", lambda v, h=h: v.tensor_copy(
                    (ev0 if h % 2 == 0 else ev1)[:], ps[2 + h % 2][:, 0:512]),
                    wait=tuple(evw), sem="v")
                kc_dma_w.append(p.dma("gpsimd", lambda e, h=h, g=g: e.dma_start(
                    out=kc_scr[h, g * 4:(g + 1) * 4].rearrange("t q n -> q t n"),
                    in_=(ev0 if h % 2 == 0 else ev1)[:].rearrange("q (t n) -> q t n", t=4)),
                    wait=(("v", p.v("v")),)))

            if "v" not in a1_stages:
                continue
            for ti in range(4):
                t = g * 4 + ti
                for a in range(4):
                    idx = ti * 4 + a
                    p.c("tensor", lambda tn, ti=ti, a=a, idx=idx: [
                        tn.matmul(ps[4 + idx % 2][:, 0:512],
                                  ckv[:, lc * 512 + ti * 128: lc * 512 + ti * 128 + 128],
                                  wuv[:, lc * 2048 + a * 512: lc * 2048 + a * 512 + 512],
                                  start=(lc == 0), stop=(lc == 3)) for lc in range(4)][-1],
                        wait=(("v", p.v("v") - (1 if idx >= 2 else 0)),), sem="t")
                    p.c("vector", lambda v, t=t, a=a, idx=idx: v.tensor_copy(
                        v_sb[:, t * 2048 + a * 512: t * 2048 + a * 512 + 512],
                        ps[4 + idx % 2][:, 0:512]),
                        wait=(("t", p.v("t")),), sem="v")

        a1_v = p.v("v")
        a1_w = p.v("w")

        def tap_out(names, wv, wq=None):
            for tname in names:
                if tname not in taps:
                    continue
                tap = taps[tname]
                if tname in ("v_sb", "kr_sb", "sin_sb", "cos_sb"):
                    smap = {"v_sb": v_sb, "kr_sb": kr_sb, "sin_sb": sin_sb, "cos_sb": cos_sb}
                    p.dma("sync", lambda e, s=smap[tname], d=tap: e.dma_start(out=d[:], in_=s[:]),
                          wait=wv, sem="o")
                else:
                    s = scr[tname]
                    p.dma("sync", lambda e, s=s, d=tap: e.dma_start(
                        out=d.flatten().rearrange("(a b) -> a b", a=128),
                        in_=s.flatten().rearrange("(a b) -> a b", a=128)),
                        wait=wv, sem="o")
        tap_out(("xn_scr", "kc_scr", "v_sb", "kr_sb", "sin_sb", "cos_sb"),
                (("v", a1_v), ("w", a1_w)))

        _open.remove(actx); actx.close()
        if end_at == "A1":
            raise _EndBuild
        a2ctx = ExitStack()
        _open.append(a2ctx)
        A2 = a2ctx.enter_context
        wdq = A2(nc.sbuf_tensor("wdq", [128, 16 * 512], BF16))
        wuq = A2(nc.sbuf_tensor("wuq", [128, 4 * 2048], BF16))
        wqr = A2(nc.sbuf_tensor("wqr", [128, 4 * 1024], BF16))
        wnat2 = A2(nc.sbuf_tensor("wnat2", [128, 8192], BF16))
        xno = A2(nc.sbuf_tensor("xno", [128, 16 * 1024], BF16))
        cq = A2(nc.sbuf_tensor("cq", [128, 4 * 1024], BF16))
        qstg = A2(nc.sbuf_tensor("qstg", [128, 1024], BF16))
        qrtm = A2(nc.sbuf_tensor("qrtm", [128, 1024], BF16))
        qrstage = A2(nc.sbuf_tensor("qrstage", [128, 1024], BF16))
        cos_own = A2(nc.sbuf_tensor("cos_own", [128, 256], F32))
        sin_own = A2(nc.sbuf_tensor("sin_own", [128, 256], F32))

        def trig_gather(e):
            last = None
            for i in range(NO):
                rv = j_rv(e, i)
                for dst, srct in ((cos_own, cos_sb), (sin_own, sin_sb)):
                    last = e.tensor_copy(
                        dst[:, i * 32:(i + 1) * 32],
                        srct[:].rearrange("p (k t) -> p k t", t=16)[:, :, ds(rv, 1)].squeeze(2))
            return last
        p.c("gpsimd", trig_gather, wait=(("q", consts_q),), sem="w2")
        trig_own_w = p.v("w2")

        wtrans(w["W_DQ"], D_LAT, D, wdq, wnat2)
        wtrans(w["W_UQ"], H * D_H, D_LAT, wuq, wnat2)
        a2w_v = wtrans(w["W_QR"], H * D_HR, D_LAT, wqr, wnat2)

        def mk_xno_dma(i):
            def f(e):
                rv = j_rv(e, i)
                return e.dma_start(
                    out=xno[:].rearrange("p (dc n) -> p dc n", n=1024)[:, :, i * 128:(i + 1) * 128],
                    in_=xn_scr[ds(rv, 1)].squeeze(0).rearrange("p (dc n) -> p dc n", dc=16))
            return f
        for i in range(NO):
            p.dma("sync", mk_xno_dma(i),
                  wait=((("w", a1_w), ("q", consts_q)) if i == 0 else ()), sem="q")
        xno_q = p.v("q")

        for lc in range(4):
            for half in range(2):
                idx = lc * 2 + half
                wt = [("q", xno_q), ("v", a2w_v)] if idx == 0 else \
                     ([("v", p.v("v"))] if idx >= 2 else [])
                p.c("tensor", lambda tn, lc=lc, half=half, idx=idx: [
                    tn.matmul(ps[idx % 2][:, 0:512],
                              wdq[:, dc * 512 + lc * 128: dc * 512 + lc * 128 + 128],
                              xno[:, dc * 1024 + half * 512: dc * 1024 + half * 512 + 512],
                              start=(dc == 0), stop=(dc == 15)) for dc in range(16)][-1],
                    wait=tuple(wt), sem="t")
                p.c("vector", lambda v, lc=lc, half=half, idx=idx: v.tensor_copy(
                    cq[:, lc * 1024 + half * 512: lc * 1024 + half * 512 + 512],
                    ps[idx % 2][:, 0:512]), wait=(("t", p.v("t")),), sem="v")
        cq_v = p.v("v")

        qc_dma_w = []
        for h in range(16):
            for half in range(2):
                wt = [("v", cq_v)] if h == 0 and half == 0 else [("v", p.v("v"))]
                p.c("tensor", lambda tn, h=h, half=half: [
                    tn.matmul(ps[2 + half][:, 0:512],
                              wuq[:, lc * 2048 + h * 128: lc * 2048 + h * 128 + 128],
                              cq[:, lc * 1024 + half * 512: lc * 1024 + half * 512 + 512],
                              start=(lc == 0), stop=(lc == 3)) for lc in range(4)][-1],
                    wait=tuple(wt), sem="t")
                evw = [("t", p.v("t"))]
                if h >= 1 and half == 0:
                    evw.append(("w", qc_dma_w[h - 1]))
                p.c("vector", lambda v, half=half: v.tensor_copy(
                    qstg[:, half * 512:(half + 1) * 512], ps[2 + half][:, 0:512]),
                    wait=tuple(evw), sem="v")
            qc_dma_w.append(p.dma("gpsimd", lambda e, h=h: e.dma_start(
                out=qc_scr[:, h].rearrange("s p n -> p s n"),
                in_=qstg[:].rearrange("p (s n) -> p s n", s=8)),
                wait=(("v", p.v("v")),)))

        qr_dma_w = []
        for i in range(NO):
            for half in range(2):
                wt = [("v", p.v("v"))]
                p.c("tensor", lambda tn, i=i, half=half: [
                    tn.matmul(ps[4 + half][:, 0:512],
                              cq[:, lc * 1024 + i * 128: lc * 1024 + i * 128 + 128],
                              wqr[:, lc * 1024 + half * 512: lc * 1024 + half * 512 + 512],
                              start=(lc == 0), stop=(lc == 3)) for lc in range(4)][-1],
                    wait=tuple(wt), sem="t")

            def rope_qr(v, i=i):
                    cosn = cos_own[:, i * 32:(i + 1) * 32].unsqueeze(1).broadcast_to([128, 8, 32])
                    sinn = sin_own[:, i * 32:(i + 1) * 32].unsqueeze(1).broadcast_to([128, 8, 32])
                    last = None
                    for half in range(2):
                        pp = ps[4 + half]
                        x1 = pp[:, 0:512:2].rearrange("p (h n) -> p h n", h=8)
                        x2 = pp[:, 1:512:2].rearrange("p (h n) -> p h n", h=8)
                        a3 = tr1[:, 0:256].rearrange("p (h n) -> p h n", h=8)
                        b3 = tr2[:, 0:256].rearrange("p (h n) -> p h n", h=8)
                        dsts = qrtm[:, half * 512:(half + 1) * 512]
                        de = dsts[:, 0:512:2].rearrange("p (h n) -> p h n", h=8)
                        do = dsts[:, 1:512:2].rearrange("p (h n) -> p h n", h=8)
                        v.tensor_mul(a3, x1, cosn)
                        v.tensor_mul(b3, x2, sinn)
                        v.drain()
                        v.tensor_sub(de, a3, b3)
                        v.tensor_mul(a3, x1, sinn)
                        v.tensor_mul(b3, x2, cosn)
                        v.drain()
                        last = v.tensor_add(do, a3, b3)
                    return last
            p.c("vector", rope_qr, wait=(("t", p.v("t")), ("w2", trig_own_w)), sem="v")
            for c in range(8):
                wt = [("v", p.v("v"))] if c >= 2 or c == 0 else []
                if c == 0 and i > 0:
                    wt.append(("w", qr_dma_w[i - 1]))
                p.c("tensor", lambda tn, c=c: tn.transpose(
                    psb[c % 2][:, (c // 2 % 4) * 128:(c // 2 % 4) * 128 + 128],
                    qrtm[:, c * 128:(c + 1) * 128], ident[:]),
                    wait=tuple(wt), sem="t")
                p.c("vector", lambda v, c=c: v.tensor_copy(
                    qrstage[:, c * 128:(c + 1) * 128],
                    psb[c % 2][:, (c // 2 % 4) * 128:(c // 2 % 4) * 128 + 128]),
                    wait=(("t", p.v("t")),), sem="v")
            p.dma("gpsimd", lambda e, i=i: e.dma_start(
                out=qr_scr[i, 0:16:2].rearrange("h p n -> p h n"),
                in_=qrstage[0:64, :].rearrange("p (h n) -> p h n", h=8)),
                wait=(("v", p.v("v")),))
            qr_dma_w.append(p.dma("gpsimd", lambda e, i=i: e.dma_start(
                out=qr_scr[i, 1:16:2].rearrange("h p n -> p h n"),
                in_=qrstage[64:128, :].rearrange("p (h n) -> p h n", h=8)),
                wait=(("v", p.v("v")),)))
        a2_w = p.v("w")
        a2_v = p.v("v")
        tap_out(("qc_scr", "qr_scr"), (("w", a2_w), ("v", a2_v)))
        p.mark["A2e"] = len(p.steps)
        _open.remove(a2ctx); a2ctx.close()
        if end_at == "A2":
            raise _EndBuild
        bctx = ExitStack()
        _open.append(bctx)
        BC = bctx.enter_context
        wo = BC(nc.sbuf_tensor("wo", [128, 16 * 2048], BF16))
        w3ctx = ExitStack()
        wnat3 = w3ctx.enter_context(nc.sbuf_tensor("wnat3", [128, 8192], BF16))
        wtrans(w["W_O"], D, H * D_H, wo, wnat3)
        wo_v = p.v("v")
        w3ctx.close()
        qcb = [BC(nc.sbuf_tensor(f"qcb{i}", [128, 16 * 128], BF16)) for i in range(2)]
        qrb = [BC(nc.sbuf_tensor(f"qrb{i}", [128, 16 * 128], BF16)) for i in range(2)]
        kcb = [BC(nc.sbuf_tensor(f"kcb{i}", [128, 16 * 128], BF16)) for i in range(2)]
        prtm = [BC(nc.sbuf_tensor(f"prtm{i}", [128, 512], BF16)) for i in range(4)]
        prT = [BC(nc.sbuf_tensor(f"prT{i}", [128, 512], BF16)) for i in range(4)]
        expb = [BC(nc.sbuf_tensor(f"expb{i}", [128, 512], F32)) for i in range(4)]
        attn_tm = BC(nc.sbuf_tensor("attn_tm", [128, 2048], BF16))
        attn_fm = BC(nc.sbuf_tensor("attn_fm", [128, 2048], BF16))
        rsb = BC(nc.sbuf_tensor("rsb", [128, 160], F32))
        opstg = BC(nc.sbuf_tensor("opstg", [128, D], F32))
        maskb = BC(nc.sbuf_tensor("maskb_sb", [128, 16 * 128], BF16))
        p.dma("sync", lambda e: e.dma_start(out=maskb[:], in_=maskb_h[:]), sem="q")

        def mk_slab_dma(i, buf, scratch, stride_h):
            def f(e):
                return e.dma_start(out=buf[:, 0:stride_h * 16] if False else buf[:],
                                   in_=scratch[i].rearrange("h p n -> p h n"))
            return f

        slot_base = [sum(PA[:i]) for i in range(NO)]
        t_scores_last = {}
        b_q_q = []
        for i in range(NO):
            wtq = [("v", wo_v)] if i == 0 else ([("t", t_scores_last[(i - 2, 15)])] if i >= 2 else [])
            p.dma("sync", lambda e, i=i: e.dma_start(
                out=qcb[i % 2][:], in_=qc_scr[i].rearrange("h p n -> p h n")),
                wait=tuple(wtq), sem="q")
            p.dma("sync", lambda e, i=i: e.dma_start(
                out=qrb[i % 2][0:64, :], in_=qr_scr[i].rearrange("h p n -> p h n")),
                sem="q")
            if i < 2:
                p.dma("sync", lambda e, i=i: e.dma_start(
                    out=qrb[i % 2][64:128, :], in_=zpad_h[:]), sem="q")
            qslab_q = p.v("q")

            kc_q = {}
            epi_v = {}
            t_tr_u = {}
            v_clamp_u = {}
            v_evac_u = {}
            t_av_u = {}
            lo_j = min(OWN_EVEN[i], OWN_ODD[i])
            g_band = lo_j // 4
            ng = (PA[i] + 3) // 4
            units = [(h, g2, min(4, PA[i] - 4 * g2))
                     for h in range(H) for g2 in range(ng)]
            SKEW = 2

            def emit_front(u, i=i, ng=ng, lo_j=lo_j, g_band=g_band, qslab_q=qslab_q):
                h, g2, nk = units[u]
                gq = u % 4
                pss = (ps[0], ps[1], ps[4], ps[5])[gq]
                if g2 == 0:
                    wtk = []
                    if h >= 2:
                        wtk.append(("t", t_scores_last[(i, h - 2)]))
                    elif i >= 1 and h < 2:
                        wtk.append(("t", t_scores_last[(i - 1, 14 + h)]))
                    p.dma("sync", lambda e, i=i, h=h: e.dma_start(
                        out=kcb[h % 2][:, 0:PA[i] * 128].rearrange("p (t n) -> p t n", n=128),
                        in_=kc_scr[h, 0:PA[i]].rearrange("t p n -> p t n")),
                        wait=tuple(wtk), sem="q")
                    kc_q[h] = p.v("q")
                wt = []
                if g2 == 0:
                    wt += [("q", kc_q[h]), ("q", qslab_q)]
                    if (i, h - 2) in epi_v:
                        wt.append(("v", epi_v[(i, h - 2)]))
                    elif h < 2 and (i - 1, 14 + h) in epi_v:
                        wt.append(("v", epi_v[(i - 1, 14 + h)]))
                    if h == 0 and i == 0:
                        wt.append(("v", wo_v))
                p.c("tensor", lambda tn, i=i, h=h, g2=g2, nk=nk, pss=pss: (
                    tn.matmul(pss[:, 0:nk * 128],
                              qcb[i % 2][:, h * 128:(h + 1) * 128],
                              kcb[h % 2][:, g2 * 512:g2 * 512 + nk * 128],
                              start=True, stop=False),
                    tn.matmul(pss[:, 0:nk * 128],
                              qrb[i % 2][0:128, h * 128:(h + 1) * 128],
                              kr_sb[0:128, g2 * 512:g2 * 512 + nk * 128],
                              start=False, stop=True))[-1],
                    wait=tuple(wt), sem="t")
                t_scores_last[(i, h)] = p.v("t")
                wt = [("t", p.v("t"))]
                if u - 4 in v_clamp_u:
                    wt.append(("v", v_clamp_u[u - 4]))
                p.c("scalar", lambda sc, gq=gq, nk=nk, pss=pss: sc.activation(
                    expb[gq][:, 0:nk * 128], pss[:, 0:nk * 128], AF.Exp,
                    scale=SCALE_QK), wait=tuple(wt), sem="s")
                wt = [("s", p.v("s"))]
                if u - 4 in t_tr_u:
                    wt.append(("t", t_tr_u[u - 4]))

                def clamp_fn(v, i=i, h=h, g2=g2, nk=nk, sb=gq, lo_j=lo_j, ng=ng, g_band=g_band):
                    v.tensor_scalar(expb[sb][:, 0:nk * 128], expb[sb][:, 0:nk * 128],
                                    CLAMP_HI, None, ALU.min)
                    if g2 != g_band:
                        return v.tensor_scalar(
                            prtm[sb][:, 0:nk * 128], expb[sb][:, 0:nk * 128],
                            CLAMP_LO, None, ALU.max, op1=ALU.add,
                            accum_out=rsb[:, h * 8 + g2:h * 8 + g2 + 1])
                    v.tensor_scalar(prtm[sb][:, 0:nk * 128], expb[sb][:, 0:nk * 128],
                                    CLAMP_LO, None, ALU.max)
                    last = None
                    nxt = ng
                    for t2 in range(nk):
                        s = 4 * g2 + t2
                        dst = prtm[sb][:, t2 * 128:(t2 + 1) * 128]
                        acc = rsb[:, h * 8 + g2:h * 8 + g2 + 1] if t2 == 0 \
                            else rsb[:, h * 8 + nxt:h * 8 + nxt + 1]
                        if t2 > 0:
                            nxt += 1
                        if s < lo_j:
                            last = v.tensor_scalar(dst, dst, 1.0, None, ALU.mult,
                                                   op1=ALU.add, accum_out=acc)
                        else:
                            k = 2 * i + (s - lo_j)
                            last = v.scalar_tensor_tensor(
                                dst, dst, 1.0, maskb[:, k * 128:(k + 1) * 128],
                                ALU.mult, ALU.mult, accum_out=acc)
                    return last
                p.c("vector", clamp_fn, wait=tuple(wt), sem="v")
                v_clamp_u[u] = p.v("v")

            def emit_back(u, i=i, ng=ng):
                h, g2, nk = units[u]
                gq = u % 4
                pb2 = u % 2
                avb = ps[2 + h % 2]
                wt = [("v", v_clamp_u[u])]
                if u - 2 in v_evac_u:
                    wt.append(("v", v_evac_u[u - 2]))
                p.c("tensor", lambda tn, gq=gq, pb2=pb2, nk=nk: [
                    tn.transpose(psb[pb2][:, t2 * 128:(t2 + 1) * 128],
                                 prtm[gq][:, t2 * 128:(t2 + 1) * 128], ident[:])
                    for t2 in range(nk)][-1],
                    wait=tuple(wt), sem="t")
                t_tr_u[u] = p.v("t")
                wt = [("t", t_tr_u[u])]
                if u - 4 in t_av_u:
                    wt.append(("t", t_av_u[u - 4]))
                p.c("vector", lambda v, gq=gq, pb2=pb2, nk=nk: v.tensor_copy(
                    prT[gq][:, 0:nk * 128], psb[pb2][:, 0:nk * 128]),
                    wait=tuple(wt), sem="v")
                v_evac_u[u] = p.v("v")
                p.c("tensor", lambda tn, i=i, h=h, g2=g2, nk=nk, gq=gq, avb=avb: [
                    tn.matmul(avb[:, 0:128], prT[gq][:, t2 * 128:(t2 + 1) * 128],
                              v_sb[:, (4 * g2 + t2) * 2048 + h * 128:
                                   (4 * g2 + t2) * 2048 + h * 128 + 128],
                              start=(g2 == 0 and t2 == 0),
                              stop=(4 * g2 + t2 == PA[i] - 1))
                    for t2 in range(nk)][-1],
                    wait=(("v", v_evac_u[u]),), sem="t")
                t_av_u[u] = p.v("t")
                if g2 == ng - 1:
                    p.c("vector", lambda v, h=h, avb=avb: v.tensor_copy(
                        attn_tm[:, h * 128:(h + 1) * 128], avb[:, 0:128]),
                        wait=(("t", t_av_u[u]),), sem="v")
                    epi_v[(i, h)] = p.v("v")

            for k in range(len(units) + SKEW):
                if k >= SKEW:
                    emit_back(k - SKEW)
                if k < len(units):
                    emit_front(k)

            nkb = min(4, PA[i] - 4 * g_band)
            nrs = ((PA[i] + 3) // 4) + max(0, nkb - 1)

            def slot_epi(v, i=i, nrs=nrs):
                v.drain()
                v.tensor_reduce(
                    rsb[:, 128:144].rearrange("p (a b) -> p a b", b=1),
                    rsb[:, 0:128].rearrange("p (a b) -> p a b", b=8)[:, :, 0:nrs],
                    axis=AX.X, op=ALU.add)
                v.drain()
                v.reciprocal(rsb[:, 144:160], rsb[:, 128:144])
                v.drain()
                last = None
                for h in range(H):
                    last = v.tensor_scalar(attn_tm[:, h * 128:(h + 1) * 128],
                                           attn_tm[:, h * 128:(h + 1) * 128],
                                           rsb[:, 144 + h:145 + h], None, ALU.mult)
                return last
            p.c("vector", slot_epi, wait=(("v", p.v("v")),), sem="v")

            attn_v = p.v("v")
            for kb in range(4):
                wt = [("v", attn_v)] if kb < 2 else [("v", p.v("v"))]
                p.c("tensor", lambda tn, kb=kb: [
                    tn.transpose(psb[kb % 2][:, q * 128:q * 128 + 128],
                                 attn_tm[:, (kb * 4 + q) * 128:(kb * 4 + q) * 128 + 128],
                                 ident[:]) for q in range(4)][-1],
                    wait=tuple(wt), sem="t")
                p.c("vector", lambda v, kb=kb: [
                    v.tensor_copy(attn_fm[:, (kb * 4 + q) * 128:(kb * 4 + q) * 128 + 128],
                                  psb[kb % 2][:, q * 128:q * 128 + 128]) for q in range(4)][-1],
                    wait=(("t", p.v("t")),), sem="v")
            afm_v = p.v("v")
            for ncol in range(4):
                wt = [("v", afm_v)] if ncol == 0 else \
                     ([("v", p.v("v"))] if ncol >= 2 else [])
                p.c("tensor", lambda tn, ncol=ncol: [
                    tn.matmul(ps[2 + ncol % 2][:, 0:512],
                              attn_fm[:, afc * 128:(afc + 1) * 128],
                              wo[:, afc * 2048 + ncol * 512: afc * 2048 + ncol * 512 + 512],
                              start=(afc == 0), stop=(afc == 15)) for afc in range(16)][-1],
                    wait=tuple(wt), sem="t")
                p.c("vector", lambda v, ncol=ncol: v.tensor_copy(
                    opstg[:, ncol * 512:(ncol + 1) * 512], ps[2 + ncol % 2][:, 0:512]),
                    wait=(("t", p.v("t")),) + ((("q", p.v("q")),) if ncol == 0 else ()),
                    sem="v")
            p.dma("sync", lambda e, i=i: e.dma_start(out=op_scr[i], in_=opstg[:]),
                  wait=(("v", p.v("v")),), sem="q")
        b_w = p.v("w")
        b_q = p.v("q")
        tap_out(("op_scr",), (("q", b_q),))
        p.mark["Be"] = len(p.steps)
        _open.remove(bctx); bctx.close()
        _open.remove(abctx); abctx.close()
        if end_at == "B":
            raise _EndBuild
        cctx = ExitStack()
        _open.append(cctx)
        CC = cctx.enter_context
        xno2 = CC(nc.sbuf_tensor("xno2", [128, 16 * 1024], BF16))
        ffn = CC(nc.sbuf_tensor("ffn", [128, 8 * 2048], F32))
        h_sb = CC(nc.sbuf_tensor("h_sb", [128, 16 * 1024], BF16))
        sil = [CC(nc.sbuf_tensor(f"sil{i}", [128, 512], BF16)) for i in range(2)]
        w1s = [CC(nc.sbuf_tensor(f"w1s{i}", [128, 16 * 128], BF16)) for i in range(2)]
        wgs = [CC(nc.sbuf_tensor(f"wgs{i}", [128, 16 * 128], BF16)) for i in range(2)]
        w2s = [CC(nc.sbuf_tensor(f"w2s{i}", [128, 16 * 512], BF16)) for i in range(2)]
        gwn = CC(nc.sbuf_tensor("gwn", [128, 16 * NE], BF16))
        glog = CC(nc.sbuf_tensor("glog", [128, 2 * NE + 4], F32))
        grk = CC(nc.sbuf_tensor("grk", [128, 49 + NE], F32))
        w_all = CC(nc.sbuf_tensor("w_all", [128, 8 * NE], F32))
        opin = CC(nc.sbuf_tensor("opin", [128, D], F32))
        xre = CC(nc.sbuf_tensor("xre", [128, D], F32))
        ystg = CC(nc.sbuf_tensor("ystg", [128, D], F32))

        def mk_xno2_dma(i):
            def f(e):
                rv = j_rv(e, i)
                return e.dma_start(
                    out=xno2[:].rearrange("p (dc n) -> p dc n", n=1024)[:, :, i * 128:(i + 1) * 128],
                    in_=xn_scr[ds(rv, 1)].squeeze(0).rearrange("p (dc n) -> p dc n", dc=16))
            return f
        for i in range(NO):
            p.dma("sync", mk_xno2_dma(i), wait=((("q", b_q),) if i == 0 else ()), sem="q")
        p.dma("gpsimd", lambda e: e.dma_start(
            out=gwn[:].rearrange("p (dc n) -> p dc n", n=NE),
            in_=w["gate_w"].rearrange("(dc p) n -> p dc n", p=128)),
            wait=(("q", b_q),))
        xno2_q = p.v("q")
        gw_w = p.v("w")

        for tc in range(NO):
            p.c("tensor", lambda tn, tc=tc: [
                tn.matmul(ps[4][:, 0:NE],
                          xno2[:, dc * 1024 + tc * 128: dc * 1024 + tc * 128 + 128],
                          gwn[:, dc * NE:(dc + 1) * NE],
                          start=(dc == 0), stop=(dc == 15)) for dc in range(16)][-1],
                wait=(("q", xno2_q), ("w", gw_w)) if tc == 0 else (("v", p.v("v")),), sem="t")

            def gate_fn(v, tc=tc):
                v.tensor_add(glog[:, 0:NE], ps[4][:, 0:NE], gateb[:])
                v.drain()
                v.reduce_max(glog[:, 2 * NE:2 * NE + 1], glog[:, 0:NE], axis=AX.X)
                v.drain()
                return v.tensor_scalar_mul(glog[:, 2 * NE + 1:2 * NE + 2], glog[:, 2 * NE:2 * NE + 1], -1.0)
            p.c("vector", gate_fn, wait=(("t", p.v("t")),), sem="v")
            p.c("scalar", lambda sc: sc.activation(
                glog[:, NE:2 * NE], glog[:, 0:NE], AF.Exp,
                bias=glog[:, 2 * NE + 1:2 * NE + 2],
                accum_out=glog[:, 2 * NE + 2:2 * NE + 3]),
                wait=(("v", p.v("v")),), sem="s")

            def gate2_fn(v, tc=tc):
                v.reciprocal(glog[:, 2 * NE + 3:2 * NE + 4], glog[:, 2 * NE + 2:2 * NE + 3])
                v.drain()
                v.tensor_scalar(glog[:, NE:2 * NE], glog[:, NE:2 * NE],
                                glog[:, 2 * NE + 3:2 * NE + 4], None, ALU.mult)
                v.drain()
                v.tensor_add(glog[:, 0:NE], glog[:, NE:2 * NE], biase[:])
                v.drain()
                a3 = glog[:, 0:NE].unsqueeze(1).broadcast_to([128, NE, NE])
                b3 = glog[:, 0:NE].unsqueeze(2).broadcast_to([128, NE, NE])
                v.tensor_tensor(grk[:, 0:49].rearrange("p (i j) -> p i j", i=NE), a3, b3, ALU.is_gt)
                v.drain()
                v.tensor_reduce(grk[:, 49:49 + NE],
                                grk[:, 0:49].rearrange("p (i j) -> p i j", i=NE),
                                axis=AX.X, op=ALU.add)
                v.drain()
                v.tensor_single_scalar(grk[:, 49:49 + NE], grk[:, 49:49 + NE], 2.5, ALU.is_lt)
                v.drain()
                v.tensor_mul(grk[:, 0:NE], glog[:, NE:2 * NE], grk[:, 49:49 + NE])
                v.drain()
                v.reduce_sum(glog[:, 2 * NE:2 * NE + 1], grk[:, 0:NE], axis=AX.X)
                v.drain()
                v.reciprocal(glog[:, 2 * NE + 1:2 * NE + 2], glog[:, 2 * NE:2 * NE + 1])
                v.drain()
                return v.tensor_scalar(w_all[:, tc * NE:(tc + 1) * NE], grk[:, 0:NE],
                                       glog[:, 2 * NE + 1:2 * NE + 2], None, ALU.mult)
            p.c("vector", gate2_fn, wait=(("s", p.v("s")),), sem="v")
        gate_v = p.v("v")

        up_dma_w = {}
        t_up_last = {}
        t_dn_last = {}
        v_mul_u = {}
        v_dn_u = {}
        uu_c = [0]
        dd_c = [0]
        for ei in range(8):
            shared = (ei == 0)
            e_idx = ei - 1
            w1h = w["Ws1"] if shared else w["W1"][e_idx]
            wgh = w["Wsg"] if shared else w["Wg"][e_idx]
            w2h = w["Ws2"] if shared else w["W2"][e_idx]
            for hc in range(16):
                bi = hc % 2
                wt = []
                if (ei, hc - 2) in t_up_last:
                    wt.append(("t", t_up_last[(ei, hc - 2)]))
                elif hc < 2 and (ei - 1, 14 + hc) in t_up_last:
                    wt.append(("t", t_up_last[(ei - 1, 14 + hc)]))
                p.dma("gpsimd", lambda e, w1h=w1h, hc=hc, bi=bi: e.dma_start(
                    out=w1s[bi][:].rearrange("p (b n) -> p b n", b=16),
                    in_=w1h[:, hc * 128:(hc + 1) * 128].rearrange("(b p) n -> p b n", p=128)),
                    wait=tuple(wt), sem="w")
                p.dma("gpsimd", lambda e, wgh=wgh, hc=hc, bi=bi: e.dma_start(
                    out=wgs[bi][:].rearrange("p (b n) -> p b n", b=16),
                    in_=wgh[:, hc * 128:(hc + 1) * 128].rearrange("(b p) n -> p b n", p=128)),
                    sem="w")
                up_w = p.v("w")
                for t2 in range(2):
                    uu = uu_c[0]
                    uu_c[0] += 1
                    pa, pb = (ps[0], ps[1]) if uu % 2 == 0 else (ps[2], ps[3])
                    wt = [("w", up_w)]
                    if ei == 0 and hc == 0 and t2 == 0:
                        wt.append(("v", gate_v))
                    if uu - 2 in v_mul_u:
                        wt.append(("v", v_mul_u[uu - 2]))
                    p.c("tensor", lambda tn, hc=hc, t2=t2, bi=bi, pa=pa: [
                        tn.matmul(pa[:, 0:512],
                                  w1s[bi][:, dc * 128:(dc + 1) * 128],
                                  xno2[:, dc * 1024 + t2 * 512: dc * 1024 + t2 * 512 + 512],
                                  start=(dc == 0), stop=(dc == 15)) for dc in range(16)][-1],
                        wait=tuple(wt), sem="t")
                    t_pa = p.v("t")
                    p.c("tensor", lambda tn, hc=hc, t2=t2, bi=bi, pb=pb: [
                        tn.matmul(pb[:, 0:512],
                                  wgs[bi][:, dc * 128:(dc + 1) * 128],
                                  xno2[:, dc * 1024 + t2 * 512: dc * 1024 + t2 * 512 + 512],
                                  start=(dc == 0), stop=(dc == 15)) for dc in range(16)][-1],
                        sem="t")
                    t_up_last[(ei, hc)] = p.v("t")
                    wt = [("t", t_pa)]
                    if uu - 2 in v_mul_u:
                        wt.append(("v", v_mul_u[uu - 2]))
                    p.c("scalar", lambda sc, uu=uu, pa=pa: sc.activation(
                        sil[uu % 2][:], pa[:, 0:512], AF.Silu),
                        wait=tuple(wt), sem="s")
                    p.c("vector", lambda v, hc=hc, t2=t2, uu=uu, pb=pb: v.tensor_mul(
                        h_sb[:, hc * 1024 + t2 * 512: hc * 1024 + t2 * 512 + 512],
                        pb[:, 0:512], sil[uu % 2][:]),
                        wait=(("s", p.v("s")), ("t", t_up_last[(ei, hc)])), sem="v")
                    v_mul_u[uu] = p.v("v")
            up_v = p.v("v")
            for dcol in range(4):
                bi = dcol % 2
                wt = []
                if (ei, dcol - 2) in t_dn_last:
                    wt.append(("t", t_dn_last[(ei, dcol - 2)]))
                elif dcol < 2 and (ei - 1, 2 + dcol) in t_dn_last:
                    wt.append(("t", t_dn_last[(ei - 1, 2 + dcol)]))
                p.dma("gpsimd", lambda e, w2h=w2h, dcol=dcol, bi=bi: e.dma_start(
                    out=w2s[bi][:].rearrange("p (b n) -> p b n", b=16),
                    in_=w2h[:, dcol * 512:(dcol + 1) * 512].rearrange("(b p) n -> p b n", p=128)),
                    wait=tuple(wt), sem="w")
                dn_w = p.v("w")
                for tc in range(NO):
                    dd = dd_c[0]
                    dd_c[0] += 1
                    pdn = ps[4 + dd % 2]
                    wt = [("w", dn_w)]
                    if dcol == 0 and tc == 0:
                        wt.append(("v", up_v))
                    if dd - 2 in v_dn_u:
                        wt.append(("v", v_dn_u[dd - 2]))
                    p.c("tensor", lambda tn, dcol=dcol, tc=tc, bi=bi, pdn=pdn: [
                        tn.matmul(pdn[:, 0:512],
                                  h_sb[:, hc * 1024 + tc * 128: hc * 1024 + tc * 128 + 128],
                                  w2s[bi][:, hc * 512:(hc + 1) * 512],
                                  start=(hc == 0), stop=(hc == 15)) for hc in range(16)][-1],
                        wait=tuple(wt), sem="t")
                    t_dn_last[(ei, dcol)] = p.v("t")
                    if shared:
                        p.c("vector", lambda v, dcol=dcol, tc=tc, pdn=pdn: v.tensor_copy(
                            ffn[:, tc * 2048 + dcol * 512: tc * 2048 + dcol * 512 + 512],
                            pdn[:, 0:512]),
                            wait=(("t", p.v("t")),), sem="v")
                    else:
                        p.c("vector", lambda v, dcol=dcol, tc=tc, pdn=pdn, e_idx=e_idx: v.scalar_tensor_tensor(
                            ffn[:, tc * 2048 + dcol * 512: tc * 2048 + dcol * 512 + 512],
                            pdn[:, 0:512], w_all[:, tc * NE + e_idx: tc * NE + e_idx + 1],
                            ffn[:, tc * 2048 + dcol * 512: tc * 2048 + dcol * 512 + 512],
                            ALU.mult, ALU.add),
                            wait=(("t", p.v("t")),), sem="v")
                    v_dn_u[dd] = p.v("v")
        ffn_v = p.v("v")

        def mk_xre_c(i):
            def f(e):
                rv = jx_rv(e, i)
                return e.dma_start(out=xre[:], in_=x_h[ds(rv, 128), :])
            return f
        for tc in range(NO):
            p.dma("sync", lambda e, tc=tc: e.dma_start(out=opin[:], in_=op_scr[tc]),
                  wait=((("v", ffn_v),) if tc == 0 else (("o", p.v("o")),)), sem="q")
            p.dma("sync", mk_xre_c(tc), sem="x")
            p.c("vector", lambda v, tc=tc: v.tensor_add(
                ystg[:], ffn[:, tc * 2048:(tc + 1) * 2048], opin[:]),
                wait=(("q", p.v("q")), ("o", p.v("o"))), sem="v")
            p.c("vector", lambda v: v.tensor_add(ystg[:], ystg[:], xre[:]),
                wait=(("x", p.v("x")),), sem="v")
            p.dma("sync", lambda e, tc=tc: e.dma_start(out=y_h[tc], in_=ystg[:]),
                  wait=(("v", p.v("v")),), sem="o")
        _open.remove(cctx); cctx.close()
        p.step("sync", None, wait=(("o", p.v("o")),))
      except _EndBuild:
        for _st in reversed(_open):
            _st.close()
      if True:
        for sname in ("x", "w", "w2", "t", "v", "s", "q", "o"):
            if p.v(sname):
                p.step("sync", None, wait=((sname, p.v(sname)),))

        mk = getattr(p, "mark", {})
        a1e = mk.get("A1e", 10**9); a2e = mk.get("A2e", 10**9); be = mk.get("Be", 10**9)

        def phase_of(idx):
            if idx < a1e:
                return "phA"
            if idx < a2e:
                return "phA2"
            if idx < be:
                return "phB"
            return "phC"

        @block.sync
        def _(e):
            p.emit("sync", e, sems, marks=phase_of)

        @block.gpsimd
        def _(e):
            p.emit("gpsimd", e, sems, marks=phase_of)

        @block.tensor
        def _(e):
            p.emit("tensor", e, sems, marks=phase_of)

        @block.vector
        def _(e):
            p.emit("vector", e, sems, marks=phase_of)

        @block.scalar
        def _(e):
            p.emit("scalar", e, sems, marks=phase_of)

    return nc


def make_consts():
    ident = np.eye(128, dtype=ml_dtypes.bfloat16)
    ones = np.ones((128, 128), dtype=np.float32)
    causal = np.tril(np.ones((128, 128), np.float32))
    zeros = np.zeros((128, 128), dtype=np.float32)
    masks = np.concatenate([ones, zeros], axis=1)
    pos = (np.arange(NT)[None, :] * 128 + np.arange(128)[:, None]).astype(np.float32)
    return ident, masks, pos, (ones, causal, zeros)


def make_mask_band(own):
    ident, masks, pos, (ones, causal, zeros) = make_consts()
    band = np.zeros((128, 16 * 128), np.float32)
    for i, j in enumerate(own):
        lo = min(OWN_EVEN[i], OWN_ODD[i])
        for b in range(2):
            s = lo + b
            tile = causal if s == j else (ones if s < j else zeros)
            band[:, (2 * i + b) * 128:(2 * i + b + 1) * 128] = tile
    return band


def make_trig_tables():
    import jax
    import jax.numpy as jnp
    with jax.default_device(jax.devices("cpu")[0]):
        inv_freq = 10000.0 ** (-jnp.arange(0, 2 * 32, 2, dtype=jnp.float32) / (2 * 32))
        ang = jnp.arange(S).astype(jnp.float32)[:, None] * inv_freq[None, :]
        cos, sin = jnp.cos(ang), jnp.sin(ang)
    def lay(a):
        return np.ascontiguousarray(
            np.asarray(a, np.float32).reshape(16, 128, 32).transpose(1, 2, 0).reshape(128, 512))
    return lay(sin), lay(cos)


def kernel(**inputs):
    x = np.asarray(inputs["L0_attn_x"], np.float32)
    ident, masks, pos, _ = make_consts()
    base = {
        "W_DQ": inputs["L0_attn_W_DQ"], "W_UQ": inputs["L0_attn_W_UQ"],
        "W_QR": inputs["L0_attn_W_QR"], "W_DKV": inputs["L0_attn_W_DKV"],
        "W_UK": inputs["L0_attn_W_UK"], "W_UV": inputs["L0_attn_W_UV"],
        "W_KR": inputs["L0_attn_W_KR"], "W_O": inputs["L0_attn_W_O"],
        "W1": inputs["L0_ffn_W1"], "Wg": inputs["L0_ffn_Wg"], "W2": inputs["L0_ffn_W2"],
        "Ws1": inputs["L0_ffn_Ws1"], "Wsg": inputs["L0_ffn_Wsg"], "Ws2": inputs["L0_ffn_Ws2"],
        "gate_w": inputs["L0_ffn_gate_w"],
        "gate_b_rep": np.ascontiguousarray(np.tile(np.asarray(inputs["L0_ffn_gate_b"], np.float32)[None, :], (128, 1))),
        "biases_rep": np.ascontiguousarray(np.tile(np.asarray(inputs["L0_ffn_biases"], np.float32)[None, :], (128, 1))),
        "ident": ident, "pos": pos,
    }
    base = {k: np.ascontiguousarray(np.asarray(v, dtype=(ml_dtypes.bfloat16 if k == "ident" else np.float32)))
            for k, v in base.items()}
    nc = build_nc()
    sin_tab, cos_tab = make_trig_tables()
    in_maps = []
    for c in range(8):
        b, par = c // 2, c % 2
        own = OWN_EVEN if par == 0 else OWN_ODD
        m = dict(base)
        m["x"] = np.ascontiguousarray(x[b])
        m["own_j"] = np.asarray([own], np.int32)
        m["mask_band"] = make_mask_band(own).astype(ml_dtypes.bfloat16)
        m["zpad"] = np.zeros((64, 16 * 128), ml_dtypes.bfloat16)
        m["sin_t"] = sin_tab
        m["cos_t"] = cos_tab
        in_maps.append(m)
    res = run_bass_kernel_spmd(nc, in_maps, core_ids=list(range(8)))
    out = np.zeros((B, S, D), np.float32)
    for c in range(8):
        b, par = c // 2, c % 2
        own = OWN_EVEN if par == 0 else OWN_ODD
        yc = res.results[c]["y"]
        for i, j in enumerate(own):
            out[b, j * 128:(j + 1) * 128, :] = yc[i]
    return out


def _mask_sel(own):
    sel = []
    for i, j in enumerate(own):
        for s in range(PA[i]):
            sel.append(0 if s < j else (128 if s == j else 256))
    return sel


# version 15
# speedup vs baseline: 1.1900x; 1.0856x over previous
import math
from contextlib import ExitStack

import numpy as np
import ml_dtypes

import concourse.bass as bass
import concourse.mybir as mybir
from concourse.bass_utils import run_bass_kernel_spmd
from concourse.expressions import RuntimeValue

F32 = mybir.dt.float32
BF16 = mybir.dt.bfloat16
I32 = mybir.dt.int32
AF = mybir.ActivationFunctionType
ALU = mybir.AluOpType
AX = mybir.AxisListType
ds = bass.ds

B, S, D = 4, 2048, 2048
H, D_H, D_HR, D_LAT = 16, 128, 64, 512
NT = S // 128
NO = 8
NE = 7
D_QK = D_H + D_HR
SQRT_DQK_F32 = float(np.float32(np.sqrt(np.float32(float(D_QK)))))
SCALE_QK = 1.0 / SQRT_DQK_F32
CLIP = 80.0
MASK_NEG = -100000.0
LN_EPS = 1e-5
CLAMP_HI = math.exp(CLIP / SQRT_DQK_F32)
CLAMP_LO = math.exp(-CLIP / SQRT_DQK_F32)

OWN_EVEN = [0, 15, 3, 12, 4, 11, 7, 8]
OWN_ODD = [1, 14, 2, 13, 5, 10, 6, 9]
PA = [max(a, b) + 1 for a, b in zip(OWN_EVEN, OWN_ODD)]
NSLOT = sum(PA)

SIN_C = [0.9999999999999537, -0.1666666666647968, 0.008333333320897147,
         -0.00019841266760649062, 2.7556957552677794e-06,
         -2.5030378131205352e-08, 1.5411754176956953e-10]
TWO_PI = 2.0 * math.pi
PI = math.pi
INVF = [10000.0 ** (-k / 32.0) for k in range(32)]
C2PI_1 = 6.283203125
C2PI_2 = -1.7814338207244873e-05
C2PI_3 = -3.482206523131026e-09

ENGINES = ("sync", "gpsimd", "tensor", "vector", "scalar")


class _EndBuild(Exception):
    pass


class Prog:

    def __init__(self):
        self.steps = []
        self.n = {}
        self.mark = {}

    def v(self, s):
        return self.n.get(s, 0)

    def step(self, eng, fn=None, wait=(), inc=()):
        for s, a in inc:
            self.n[s] = self.n.get(s, 0) + a
        self.steps.append((eng, fn, tuple(wait), tuple(inc)))

    def dma(self, eng, fn, wait=(), sem="w"):
        self.step(eng, fn, wait, ((sem, 16),))
        return self.v(sem)

    def c(self, eng, fn, wait=(), sem=None):
        inc = ((sem, 1),) if sem else ()
        self.step(eng, fn, wait, inc)
        return self.v(sem) if sem else None

    def emit(self, name, eng, sems, marks=None):
        nc_b = getattr(eng, "bass", None)
        cur = None
        for idx, (e, fn, wait, inc) in enumerate(self.steps):
            if marks is not None:
                ph = marks(idx)
                if ph != cur and nc_b is not None:
                    if cur is not None:
                        nc_b.pop_named_scope(cur)
                    if ph is not None:
                        nc_b.push_named_scope(ph)
                    cur = ph
            if e != name:
                continue
            for s, v in wait:
                eng.wait_ge(sems[s], v)
            ins = fn(eng) if fn is not None else None
            if inc:
                assert ins is not None and hasattr(ins, "then_inc"), (fn, inc)
                assert len(inc) == 1
                (s0, a0) = inc[0]
                ins.then_inc(sems[s0], a0)
        if marks is not None and cur is not None and nc_b is not None:
            nc_b.pop_named_scope(cur)


def build_nc(debug_taps=(), end_at="full", a1_stages=("xn", "ckv", "kr", "kc", "v"), a1_groups=4, pre_stages=("trig", "wtrans")):
    nc = bass.Bass("TRN2", target_bir_lowering=False, detect_race_conditions=False)

    x_h = nc.dram_tensor("x", [S, D], F32, kind="ExternalInput").ap()
    w = {}
    for name, shape in [
        ("W_DQ", [D_LAT, D]), ("W_UQ", [H * D_H, D_LAT]), ("W_QR", [H * D_HR, D_LAT]),
        ("W_DKV", [D_LAT, D]), ("W_UK", [H * D_H, D_LAT]), ("W_UV", [H * D_H, D_LAT]),
        ("W_KR", [D_HR, D]), ("W_O", [D, H * D_H]),
        ("W1", [NE, D, D]), ("Wg", [NE, D, D]), ("W2", [NE, D, D]),
        ("Ws1", [D, D]), ("Wsg", [D, D]), ("Ws2", [D, D]), ("gate_w", [D, NE]),
    ]:
        w[name] = nc.dram_tensor(name, shape, F32, kind="ExternalInput").ap()
    gate_b_h = nc.dram_tensor("gate_b_rep", [128, NE], F32, kind="ExternalInput").ap()
    biases_h = nc.dram_tensor("biases_rep", [128, NE], F32, kind="ExternalInput").ap()
    ident_h = nc.dram_tensor("ident", [128, 128], BF16, kind="ExternalInput").ap()
    maskb_h = nc.dram_tensor("mask_band", [128, 16 * 128], BF16, kind="ExternalInput").ap()
    zpad_h = nc.dram_tensor("zpad", [64, 16 * 128], BF16, kind="ExternalInput").ap()
    sin_th = nc.dram_tensor("sin_t", [128, 512], F32, kind="ExternalInput").ap()
    cos_th = nc.dram_tensor("cos_t", [128, 512], F32, kind="ExternalInput").ap()
    pos_h = nc.dram_tensor("pos", [128, NT], F32, kind="ExternalInput").ap()
    ownj_h = nc.dram_tensor("own_j", [1, NO], I32, kind="ExternalInput").ap()

    y_h = nc.dram_tensor("y", [NO, 128, D], F32, kind="ExternalOutput").ap()

    xn_scr = nc.dram_tensor("xn_scr", [NT, 128, D], BF16, kind="Internal").ap()
    kc_scr = nc.dram_tensor("kc_scr", [H, NT, 128, 128], BF16, kind="Internal").ap()
    qc_scr = nc.dram_tensor("qc_scr", [NO, H, 128, 128], BF16, kind="Internal").ap()
    qr_scr = nc.dram_tensor("qr_scr", [NO, H, 64, 128], BF16, kind="Internal").ap()
    op_scr = nc.dram_tensor("op_scr", [NO, 128, D], F32, kind="Internal").ap()
    scr = {"xn_scr": xn_scr, "kc_scr": kc_scr, "qc_scr": qc_scr,
           "qr_scr": qr_scr, "op_scr": op_scr}
    tap_shapes = {"xn_scr": ([NT, 128, D], BF16), "kc_scr": ([H, NT, 128, 128], BF16),
                  "qc_scr": ([NO, H, 128, 128], BF16), "qr_scr": ([NO, H, 64, 128], BF16),
                  "op_scr": ([NO, 128, D], F32),
                  "v_sb": ([128, NT * 2048], BF16), "kr_sb": ([128, S], BF16),
                  "sin_sb": ([128, 512], F32), "cos_sb": ([128, 512], F32)}
    taps = {t: nc.dram_tensor("tap_" + t, tap_shapes[t][0], tap_shapes[t][1],
                              kind="ExternalOutput").ap() for t in debug_taps}

    p = Prog()
    ctx = ExitStack()
    with ctx:
      _open = []
      try:
        E = ctx.enter_context

        ident = E(nc.sbuf_tensor("ident_sb", [128, 128], BF16))
        pos_sb = E(nc.sbuf_tensor("pos_sb", [128, NT], F32))
        ownj = E(nc.sbuf_tensor("ownj_sb", [1, NO], I32))
        gateb = E(nc.sbuf_tensor("gateb_sb", [128, NE], F32))
        biase = E(nc.sbuf_tensor("biase_sb", [128, NE], F32))
        abctx = ExitStack()
        _open.append(abctx)
        AB = abctx.enter_context
        sin_sb = AB(nc.sbuf_tensor("sin_sb", [128, 512], F32))
        cos_sb = AB(nc.sbuf_tensor("cos_sb", [128, 512], F32))
        v_sb = AB(nc.sbuf_tensor("v_sb", [128, NT * 2048], BF16))
        kr_sb = AB(nc.sbuf_tensor("kr_sb", [128, S], BF16))

        tr1 = AB(nc.sbuf_tensor("tr1", [128, 512], F32))
        tr2 = AB(nc.sbuf_tensor("tr2", [128, 512], F32))

        ps = [E(nc.psum_tensor(f"ps{i}", [128, 512], F32)) for i in range(6)]
        psb = [E(nc.psum_tensor(f"psb{i}", [128, 512], BF16)) for i in range(2)]

        sems = {n: E(nc.semaphore("sem_" + n)) for n in
                ["x", "w", "t", "v", "s", "q", "o", "w2"]}

        block = E(nc.Block())
        _regs = {}

        def j_rv(e, i):
            key = (e.engine, "j", i)
            if key not in _regs:
                r = e.alloc_register(f"j{i}")
                e.reg_load(r, ownj[0:1, i:i + 1])
                _regs[key] = r
            return RuntimeValue(_regs[key], 0, 15)

        def jx_rv(e, i):
            key = (e.engine, "jx", i)
            if key not in _regs:
                r = e.alloc_register(f"jx{i}")
                e.reg_load(r, ownj[0:1, i:i + 1])
                e.reg_mul(r, r, 128)
                _regs[key] = r
            return RuntimeValue(_regs[key], 0, 1920)


        for dst, src in [(sin_sb, sin_th), (cos_sb, cos_th), (ident, ident_h),
                         (pos_sb, pos_h), (ownj, ownj_h), (gateb, gate_b_h),
                         (biase, biases_h)]:
            p.dma("sync", lambda e, d=dst, s=src: e.dma_start(out=d[:], in_=s[:]), sem="q")
        consts_q = p.v("q")

        trig_v = 0

        actx = ExitStack()
        _open.append(actx)
        AC = actx.enter_context
        wdkv = AC(nc.sbuf_tensor("wdkv", [128, 16 * 512], BF16))
        wuk = AC(nc.sbuf_tensor("wuk", [128, 4 * 2048], BF16))
        wuv = AC(nc.sbuf_tensor("wuv", [128, 4 * 2048], BF16))
        wkr = AC(nc.sbuf_tensor("wkr", [128, 16 * 64], BF16))
        wnat = AC(nc.sbuf_tensor("wnat", [128, 8192], BF16))

        def wtrans(nat_h, nrows, ncols, dst_sb, wnat):
            rb = max(1, nrows // 128)
            prows = min(nrows, 128)
            cb = ncols // 128
            piece = max(1, min(rb, 8192 // ncols))
            for b0 in range(0, rb, piece):
                bn = min(piece, rb - b0)
                nat_ap = nat_h[b0 * 128:(b0 * 128 + bn * prows), :]
                if nrows >= 128:
                    nat_ap = nat_ap.rearrange("(b q) n -> q b n", q=128)
                else:
                    nat_ap = nat_ap.unsqueeze(1)
                wq = p.dma("gpsimd", lambda e, a=nat_ap, bn=bn: e.dma_start(
                    out=wnat[0:prows, 0:bn * ncols].rearrange("q (b n) -> q b n", b=bn), in_=a),
                    wait=(("t", p.v("t")),))
                tiles = [(a, b1) for a in range(cb) for b1 in range(bn)]
                v_base = p.v("v")
                for nb0 in range(0, len(tiles), 4):
                    batch = tiles[nb0:nb0 + 4]
                    bank = (nb0 // 4) % 2
                    if nb0 == 0:
                        wt = (("w", wq), ("v", v_base))
                    elif nb0 >= 8:
                        wt = (("v", p.v("v") - 1),)
                    else:
                        wt = ()
                    p.c("tensor", lambda t, batch=batch, bank=bank: [t.transpose(
                        psb[bank][0:128, k * 128:k * 128 + prows],
                        wnat[0:prows, b1 * ncols + a * 128: b1 * ncols + a * 128 + 128],
                        ident[0:prows, 0:prows]) for k, (a, b1) in enumerate(batch)][-1],
                        wait=wt, sem="t")
                    if prows == 128 and all(
                            (batch[k][0] == batch[0][0] and batch[k][1] == batch[0][1] + k)
                            for k in range(len(batch))):
                        a0, b10 = batch[0]
                        p.c("vector", lambda v, a0=a0, b10=b10, b0=b0, bank=bank, nkb=len(batch): v.tensor_copy(
                            dst_sb[:, a0 * nrows + (b0 + b10) * 128:
                                   a0 * nrows + (b0 + b10) * 128 + nkb * 128],
                            psb[bank][0:128, 0:nkb * 128]),
                            wait=(("t", p.v("t")),), sem="v")
                    else:
                        p.c("vector", lambda v, batch=batch, bank=bank, b0=b0: [v.tensor_copy(
                            dst_sb[:, a * nrows + (b0 + b1) * 128: a * nrows + (b0 + b1) * 128 + prows],
                            psb[bank][0:128, k * 128:k * 128 + prows]) for k, (a, b1) in enumerate(batch)][-1],
                            wait=(("t", p.v("t")),), sem="v")
            return p.v("v")

        p.step("gpsimd", None, wait=(("q", consts_q),))
        p.step("tensor", None, wait=(("q", consts_q),))
        if "wtrans" in pre_stages:
            wtrans(w["W_DKV"], D_LAT, D, wdkv, wnat)
            wtrans(w["W_UK"], H * D_H, D_LAT, wuk, wnat)
            wtrans(w["W_UV"], H * D_H, D_LAT, wuv, wnat)
            kvw_v = wtrans(w["W_KR"], D_HR, D, wkr, wnat)
        else:
            kvw_v = p.v("v")

        p.dma("sync", lambda e: e.dma_start(out=kr_sb[64:128, :], in_=zpad_h[:]), sem="q")
        x_bufs = [AC(nc.sbuf_tensor(f"xb{i}", [128, D], F32)) for i in range(2)]
        xn_tm = AC(nc.sbuf_tensor("xn_tm", [128, D], BF16))
        xc_sb = AC(nc.sbuf_tensor("xc_sb", [128, D], F32))
        sq_sb = AC(nc.sbuf_tensor("sq_sb", [128, D], F32))
        ln_m = AC(nc.sbuf_tensor("ln_m", [128, 8], F32))
        xn_fm = AC(nc.sbuf_tensor("xn_fm", [128, 16 * 512], BF16))
        ckv = AC(nc.sbuf_tensor("ckv", [128, 4 * 512], BF16))
        ev0 = AC(nc.sbuf_tensor("ev0", [128, 512], BF16))
        ev1 = AC(nc.sbuf_tensor("ev1", [128, 512], BF16))
        krst = AC(nc.sbuf_tensor("krst", [128, 64], BF16))

        for t in range(2):
            p.dma("sync", lambda e, t=t: e.dma_start(out=x_bufs[t][:], in_=x_h[t * 128:(t + 1) * 128, :]),
                  sem="x")

        kc_dma_w = []
        xn_dma_w = {}

        for g in range(a1_groups):
            for ti in range(4):
                t = g * 4 + ti
                xb = x_bufs[t % 2]
                p.c("vector", lambda v, xb=xb: (
                    v.reduce_sum(ln_m[:, 0:1], xb[:], axis=AX.X),
                    v.drain(),
                    v.tensor_scalar_mul(ln_m[:, 1:2], ln_m[:, 0:1], 1.0 / D),
                    v.drain(),
                    v.tensor_scalar(xc_sb[:], xb[:], ln_m[:, 1:2], None, ALU.subtract))[-1],
                    wait=(("x", 16 * (t + 1)),), sem="v")
                p.c("scalar", lambda sc: sc.activation(
                    sq_sb[:], xc_sb[:], AF.Square, accum_out=ln_m[:, 2:3]),
                    wait=(("v", p.v("v")),), sem="s")
                p.c("vector", lambda v: v.tensor_scalar(
                    ln_m[:, 3:4], ln_m[:, 2:3], 1.0 / D, LN_EPS, ALU.mult, ALU.add),
                    wait=(("s", p.v("s")),), sem="v")
                p.c("scalar", lambda sc: sc.activation(ln_m[:, 4:5], ln_m[:, 3:4], AF.Sqrt),
                    wait=(("v", p.v("v")),), sem="s")
                p.c("vector", lambda v: (
                    v.reciprocal(ln_m[:, 5:6], ln_m[:, 4:5]),
                    v.drain(),
                    v.tensor_scalar(xn_tm[:], xc_sb[:], ln_m[:, 5:6], None, ALU.mult))[-1],
                    wait=(("s", p.v("s")), ("t", p.v("t"))), sem="v")
                xn_tm_v = p.v("v")
                if t + 2 < NT:
                    p.dma("sync", lambda e, t=t: e.dma_start(
                        out=x_bufs[t % 2][:], in_=x_h[(t + 2) * 128:(t + 3) * 128, :]),
                        wait=(("v", xn_tm_v),), sem="x")
                for kbatch in range(4):
                    wt = [("v", xn_tm_v)]
                    if kbatch == 0 and (t - 4) in xn_dma_w:
                        wt.append(("w", xn_dma_w[t - 4]))
                    p.c("tensor", lambda tn, kb=kbatch: [
                        tn.transpose(psb[kb % 2][:, q * 128:q * 128 + 128],
                                     xn_tm[:, (kb * 4 + q) * 128:(kb * 4 + q) * 128 + 128],
                                     ident[:]) for q in range(4)][-1],
                        wait=tuple(wt) + ((("v", p.v("v")),) if kbatch >= 2 else ()),
                        sem="t")
                    p.c("vector", lambda v, kb=kbatch, ti=ti: [
                        v.tensor_copy(xn_fm[:, (kb * 4 + q) * 512 + ti * 128:(kb * 4 + q) * 512 + ti * 128 + 128],
                                      psb[kb % 2][:, q * 128:q * 128 + 128]) for q in range(4)][-1],
                        wait=(("t", p.v("t")),), sem="v")
                xn_dma_w[t] = p.dma("gpsimd", lambda e, t=t, ti=ti: e.dma_start(
                    out=xn_scr[t].rearrange("q (dc n) -> q dc n", dc=16),
                    in_=xn_fm[:].rearrange("q (dc n) -> q dc n", dc=16)[:, :, ti * 128:ti * 128 + 128]),
                    wait=(("v", p.v("v")),))

            xn_group_v = p.v("v")
            if "ckv" not in a1_stages:
                continue
            for lc in range(4):
                wt = [("v", xn_group_v), ("w", p.v("w"))] if lc == 0 else \
                     ([("v", p.v("v"))] if lc >= 2 else [])
                p.c("tensor", lambda tn, lc=lc: [
                    tn.matmul(ps[lc % 2][:, 0:512],
                              wdkv[:, dc * 512 + lc * 128: dc * 512 + lc * 128 + 128],
                              xn_fm[:, dc * 512: dc * 512 + 512],
                              start=(dc == 0), stop=(dc == 15)) for dc in range(16)][-1],
                    wait=tuple(wt), sem="t")
                p.c("vector", lambda v, lc=lc: v.tensor_copy(
                    ckv[:, lc * 512:(lc + 1) * 512], ps[lc % 2][:, 0:512]),
                    wait=(("t", p.v("t")),), sem="v")
            ckv_v = p.v("v")
            if "kr" not in a1_stages:
                continue
            for ti in range(4):
                t = g * 4 + ti
                p.c("tensor", lambda tn, ti=ti: [
                    tn.matmul(ps[2][0:128, 0:64],
                              xn_fm[:, dc * 512 + ti * 128: dc * 512 + ti * 128 + 128],
                              wkr[:, dc * 64:(dc + 1) * 64],
                              start=(dc == 0), stop=(dc == 15)) for dc in range(16)][-1],
                    wait=((("v", p.v("v")),) if ti > 0 else (("v", ckv_v),)), sem="t")

                def rope_kr(v, t=t):
                    x1 = ps[2][:, 0:64:2]
                    x2 = ps[2][:, 1:64:2]
                    cosn = cos_sb[:, t:512:16]
                    sinn = sin_sb[:, t:512:16]
                    a = tr1[:, 0:32]
                    b = tr2[:, 0:32]
                    v.tensor_mul(a, x1, cosn)
                    v.tensor_mul(b, x2, sinn)
                    v.drain()
                    v.tensor_sub(krst[:, 0:64:2], a, b)
                    v.tensor_mul(a, x1, sinn)
                    v.tensor_mul(b, x2, cosn)
                    v.drain()
                    return v.tensor_add(krst[:, 1:64:2], a, b)
                p.c("vector", rope_kr, wait=(("t", p.v("t")),), sem="v")
                p.c("tensor", lambda tn: tn.transpose(psb[0][0:64, 0:128], krst[:], ident[:]),
                    wait=(("v", p.v("v")),), sem="t")
                p.c("vector", lambda v, t=t: v.tensor_copy(
                    kr_sb[0:64, t * 128:(t + 1) * 128], psb[0][0:64, 0:128]),
                    wait=(("t", p.v("t")),), sem="v")

            if "kc" not in a1_stages:
                continue
            for h in range(16):
                wt = [("v", p.v("v"))] if h < 2 else [("v", p.v("v") - 1)]
                p.c("tensor", lambda tn, h=h: [
                    tn.matmul(ps[2 + h % 2][:, 0:512],
                              wuk[:, lc * 2048 + h * 128: lc * 2048 + h * 128 + 128],
                              ckv[:, lc * 512: lc * 512 + 512],
                              start=(lc == 0), stop=(lc == 3)) for lc in range(4)][-1],
                    wait=tuple(wt), sem="t")
                evw = [("t", p.v("t"))]
                if len(kc_dma_w) >= 2:
                    evw.append(("w", kc_dma_w[-2]))
                p.c("vector", lambda v, h=h: v.tensor_copy(
                    (ev0 if h % 2 == 0 else ev1)[:], ps[2 + h % 2][:, 0:512]),
                    wait=tuple(evw), sem="v")
                kc_dma_w.append(p.dma("gpsimd", lambda e, h=h, g=g: e.dma_start(
                    out=kc_scr[h, g * 4:(g + 1) * 4].rearrange("t q n -> q t n"),
                    in_=(ev0 if h % 2 == 0 else ev1)[:].rearrange("q (t n) -> q t n", t=4)),
                    wait=(("v", p.v("v")),)))

            if "v" not in a1_stages:
                continue
            for ti in range(4):
                t = g * 4 + ti
                for a in range(4):
                    idx = ti * 4 + a
                    p.c("tensor", lambda tn, ti=ti, a=a, idx=idx: [
                        tn.matmul(ps[4 + idx % 2][:, 0:512],
                                  ckv[:, lc * 512 + ti * 128: lc * 512 + ti * 128 + 128],
                                  wuv[:, lc * 2048 + a * 512: lc * 2048 + a * 512 + 512],
                                  start=(lc == 0), stop=(lc == 3)) for lc in range(4)][-1],
                        wait=(("v", p.v("v") - (1 if idx >= 2 else 0)),), sem="t")
                    p.c("vector", lambda v, t=t, a=a, idx=idx: v.tensor_copy(
                        v_sb[:, t * 2048 + a * 512: t * 2048 + a * 512 + 512],
                        ps[4 + idx % 2][:, 0:512]),
                        wait=(("t", p.v("t")),), sem="v")

        a1_v = p.v("v")
        a1_w = p.v("w")

        def tap_out(names, wv, wq=None):
            for tname in names:
                if tname not in taps:
                    continue
                tap = taps[tname]
                if tname in ("v_sb", "kr_sb", "sin_sb", "cos_sb"):
                    smap = {"v_sb": v_sb, "kr_sb": kr_sb, "sin_sb": sin_sb, "cos_sb": cos_sb}
                    p.dma("sync", lambda e, s=smap[tname], d=tap: e.dma_start(out=d[:], in_=s[:]),
                          wait=wv, sem="o")
                else:
                    s = scr[tname]
                    p.dma("sync", lambda e, s=s, d=tap: e.dma_start(
                        out=d.flatten().rearrange("(a b) -> a b", a=128),
                        in_=s.flatten().rearrange("(a b) -> a b", a=128)),
                        wait=wv, sem="o")
        tap_out(("xn_scr", "kc_scr", "v_sb", "kr_sb", "sin_sb", "cos_sb"),
                (("v", a1_v), ("w", a1_w)))

        _open.remove(actx); actx.close()
        if end_at == "A1":
            raise _EndBuild
        a2ctx = ExitStack()
        _open.append(a2ctx)
        A2 = a2ctx.enter_context
        wdq = A2(nc.sbuf_tensor("wdq", [128, 16 * 512], BF16))
        wuq = A2(nc.sbuf_tensor("wuq", [128, 4 * 2048], BF16))
        wqr = A2(nc.sbuf_tensor("wqr", [128, 4 * 1024], BF16))
        wnat2 = A2(nc.sbuf_tensor("wnat2", [128, 8192], BF16))
        xno = A2(nc.sbuf_tensor("xno", [128, 16 * 1024], BF16))
        cq = A2(nc.sbuf_tensor("cq", [128, 4 * 1024], BF16))
        qstg = A2(nc.sbuf_tensor("qstg", [128, 1024], BF16))
        qrtm = A2(nc.sbuf_tensor("qrtm", [128, 1024], BF16))
        qrstage = A2(nc.sbuf_tensor("qrstage", [128, 1024], BF16))
        cos_own = A2(nc.sbuf_tensor("cos_own", [128, 256], F32))
        sin_own = A2(nc.sbuf_tensor("sin_own", [128, 256], F32))

        def trig_gather(e):
            last = None
            for i in range(NO):
                rv = j_rv(e, i)
                for dst, srct in ((cos_own, cos_sb), (sin_own, sin_sb)):
                    last = e.tensor_copy(
                        dst[:, i * 32:(i + 1) * 32],
                        srct[:].rearrange("p (k t) -> p k t", t=16)[:, :, ds(rv, 1)].squeeze(2))
            return last
        p.c("gpsimd", trig_gather, wait=(("q", consts_q),), sem="w2")
        trig_own_w = p.v("w2")

        wtrans(w["W_DQ"], D_LAT, D, wdq, wnat2)
        wtrans(w["W_UQ"], H * D_H, D_LAT, wuq, wnat2)
        a2w_v = wtrans(w["W_QR"], H * D_HR, D_LAT, wqr, wnat2)

        def mk_xno_dma(i):
            def f(e):
                rv = j_rv(e, i)
                return e.dma_start(
                    out=xno[:].rearrange("p (dc n) -> p dc n", n=1024)[:, :, i * 128:(i + 1) * 128],
                    in_=xn_scr[ds(rv, 1)].squeeze(0).rearrange("p (dc n) -> p dc n", dc=16))
            return f
        for i in range(NO):
            p.dma("sync", mk_xno_dma(i),
                  wait=((("w", a1_w), ("q", consts_q)) if i == 0 else ()), sem="q")
        xno_q = p.v("q")

        for lc in range(4):
            for half in range(2):
                idx = lc * 2 + half
                wt = [("q", xno_q), ("v", a2w_v)] if idx == 0 else \
                     ([("v", p.v("v"))] if idx >= 2 else [])
                p.c("tensor", lambda tn, lc=lc, half=half, idx=idx: [
                    tn.matmul(ps[idx % 2][:, 0:512],
                              wdq[:, dc * 512 + lc * 128: dc * 512 + lc * 128 + 128],
                              xno[:, dc * 1024 + half * 512: dc * 1024 + half * 512 + 512],
                              start=(dc == 0), stop=(dc == 15)) for dc in range(16)][-1],
                    wait=tuple(wt), sem="t")
                p.c("vector", lambda v, lc=lc, half=half, idx=idx: v.tensor_copy(
                    cq[:, lc * 1024 + half * 512: lc * 1024 + half * 512 + 512],
                    ps[idx % 2][:, 0:512]), wait=(("t", p.v("t")),), sem="v")
        cq_v = p.v("v")

        qc_dma_w = []
        for h in range(16):
            for half in range(2):
                wt = [("v", cq_v)] if h == 0 and half == 0 else [("v", p.v("v"))]
                p.c("tensor", lambda tn, h=h, half=half: [
                    tn.matmul(ps[2 + half][:, 0:512],
                              wuq[:, lc * 2048 + h * 128: lc * 2048 + h * 128 + 128],
                              cq[:, lc * 1024 + half * 512: lc * 1024 + half * 512 + 512],
                              start=(lc == 0), stop=(lc == 3)) for lc in range(4)][-1],
                    wait=tuple(wt), sem="t")
                evw = [("t", p.v("t"))]
                if h >= 1 and half == 0:
                    evw.append(("w", qc_dma_w[h - 1]))
                p.c("vector", lambda v, half=half: v.tensor_copy(
                    qstg[:, half * 512:(half + 1) * 512], ps[2 + half][:, 0:512]),
                    wait=tuple(evw), sem="v")
            qc_dma_w.append(p.dma("gpsimd", lambda e, h=h: e.dma_start(
                out=qc_scr[:, h].rearrange("s p n -> p s n"),
                in_=qstg[:].rearrange("p (s n) -> p s n", s=8)),
                wait=(("v", p.v("v")),)))

        qr_dma_w = []
        for i in range(NO):
            for half in range(2):
                wt = [("v", p.v("v"))]
                p.c("tensor", lambda tn, i=i, half=half: [
                    tn.matmul(ps[4 + half][:, 0:512],
                              cq[:, lc * 1024 + i * 128: lc * 1024 + i * 128 + 128],
                              wqr[:, lc * 1024 + half * 512: lc * 1024 + half * 512 + 512],
                              start=(lc == 0), stop=(lc == 3)) for lc in range(4)][-1],
                    wait=tuple(wt), sem="t")

            def rope_qr(v, i=i):
                    cosn = cos_own[:, i * 32:(i + 1) * 32].unsqueeze(1).broadcast_to([128, 8, 32])
                    sinn = sin_own[:, i * 32:(i + 1) * 32].unsqueeze(1).broadcast_to([128, 8, 32])
                    last = None
                    for half in range(2):
                        pp = ps[4 + half]
                        x1 = pp[:, 0:512:2].rearrange("p (h n) -> p h n", h=8)
                        x2 = pp[:, 1:512:2].rearrange("p (h n) -> p h n", h=8)
                        a3 = tr1[:, 0:256].rearrange("p (h n) -> p h n", h=8)
                        b3 = tr2[:, 0:256].rearrange("p (h n) -> p h n", h=8)
                        dsts = qrtm[:, half * 512:(half + 1) * 512]
                        de = dsts[:, 0:512:2].rearrange("p (h n) -> p h n", h=8)
                        do = dsts[:, 1:512:2].rearrange("p (h n) -> p h n", h=8)
                        v.tensor_mul(a3, x1, cosn)
                        v.tensor_mul(b3, x2, sinn)
                        v.drain()
                        v.tensor_sub(de, a3, b3)
                        v.tensor_mul(a3, x1, sinn)
                        v.tensor_mul(b3, x2, cosn)
                        v.drain()
                        last = v.tensor_add(do, a3, b3)
                    return last
            p.c("vector", rope_qr, wait=(("t", p.v("t")), ("w2", trig_own_w)), sem="v")
            for c in range(8):
                wt = [("v", p.v("v"))] if c >= 2 or c == 0 else []
                if c == 0 and i > 0:
                    wt.append(("w", qr_dma_w[i - 1]))
                p.c("tensor", lambda tn, c=c: tn.transpose(
                    psb[c % 2][:, (c // 2 % 4) * 128:(c // 2 % 4) * 128 + 128],
                    qrtm[:, c * 128:(c + 1) * 128], ident[:]),
                    wait=tuple(wt), sem="t")
                p.c("vector", lambda v, c=c: v.tensor_copy(
                    qrstage[:, c * 128:(c + 1) * 128],
                    psb[c % 2][:, (c // 2 % 4) * 128:(c // 2 % 4) * 128 + 128]),
                    wait=(("t", p.v("t")),), sem="v")
            p.dma("gpsimd", lambda e, i=i: e.dma_start(
                out=qr_scr[i, 0:16:2].rearrange("h p n -> p h n"),
                in_=qrstage[0:64, :].rearrange("p (h n) -> p h n", h=8)),
                wait=(("v", p.v("v")),))
            qr_dma_w.append(p.dma("gpsimd", lambda e, i=i: e.dma_start(
                out=qr_scr[i, 1:16:2].rearrange("h p n -> p h n"),
                in_=qrstage[64:128, :].rearrange("p (h n) -> p h n", h=8)),
                wait=(("v", p.v("v")),)))
        a2_w = p.v("w")
        a2_v = p.v("v")
        tap_out(("qc_scr", "qr_scr"), (("w", a2_w), ("v", a2_v)))
        p.mark["A2e"] = len(p.steps)
        _open.remove(a2ctx); a2ctx.close()
        if end_at == "A2":
            raise _EndBuild
        bctx = ExitStack()
        _open.append(bctx)
        BC = bctx.enter_context
        wo = BC(nc.sbuf_tensor("wo", [128, 16 * 2048], BF16))
        w3ctx = ExitStack()
        wnat3 = w3ctx.enter_context(nc.sbuf_tensor("wnat3", [128, 8192], BF16))
        wtrans(w["W_O"], D, H * D_H, wo, wnat3)
        wo_v = p.v("v")
        w3ctx.close()
        qcb = [BC(nc.sbuf_tensor(f"qcb{i}", [128, 16 * 128], BF16)) for i in range(2)]
        qrb = [BC(nc.sbuf_tensor(f"qrb{i}", [128, 16 * 128], BF16)) for i in range(2)]
        kcb = [BC(nc.sbuf_tensor(f"kcb{i}", [128, 16 * 128], BF16)) for i in range(2)]
        prtm = [BC(nc.sbuf_tensor(f"prtm{i}", [128, 512], BF16)) for i in range(4)]
        prT = [BC(nc.sbuf_tensor(f"prT{i}", [128, 512], BF16)) for i in range(4)]
        expb = [BC(nc.sbuf_tensor(f"expb{i}", [128, 512], F32)) for i in range(4)]
        attn_tm = BC(nc.sbuf_tensor("attn_tm", [128, 2048], BF16))
        attn_fm = BC(nc.sbuf_tensor("attn_fm", [128, 2048], BF16))
        rsb = BC(nc.sbuf_tensor("rsb", [128, 160], F32))
        opstg = BC(nc.sbuf_tensor("opstg", [128, D], F32))
        maskb = BC(nc.sbuf_tensor("maskb_sb", [128, 16 * 128], BF16))
        p.dma("sync", lambda e: e.dma_start(out=maskb[:], in_=maskb_h[:]), sem="q")

        def mk_slab_dma(i, buf, scratch, stride_h):
            def f(e):
                return e.dma_start(out=buf[:, 0:stride_h * 16] if False else buf[:],
                                   in_=scratch[i].rearrange("h p n -> p h n"))
            return f

        slot_base = [sum(PA[:i]) for i in range(NO)]
        t_scores_last = {}
        b_q_q = []
        for i in range(NO):
            wtq = [("v", wo_v)] if i == 0 else ([("t", t_scores_last[(i - 2, 15)])] if i >= 2 else [])
            p.dma("sync", lambda e, i=i: e.dma_start(
                out=qcb[i % 2][:], in_=qc_scr[i].rearrange("h p n -> p h n")),
                wait=tuple(wtq), sem="q")
            p.dma("sync", lambda e, i=i: e.dma_start(
                out=qrb[i % 2][0:64, :], in_=qr_scr[i].rearrange("h p n -> p h n")),
                sem="q")
            if i < 2:
                p.dma("sync", lambda e, i=i: e.dma_start(
                    out=qrb[i % 2][64:128, :], in_=zpad_h[:]), sem="q")
            qslab_q = p.v("q")

            kc_q = {}
            epi_v = {}
            t_tr_u = {}
            v_clamp_u = {}
            v_evac_u = {}
            t_av_u = {}
            lo_j = min(OWN_EVEN[i], OWN_ODD[i])
            g_band = lo_j // 4
            ng = (PA[i] + 3) // 4
            units = [(h, g2, min(4, PA[i] - 4 * g2))
                     for h in range(H) for g2 in range(ng)]
            SKEW = 2

            def emit_front(u, i=i, ng=ng, lo_j=lo_j, g_band=g_band, qslab_q=qslab_q):
                h, g2, nk = units[u]
                gq = u % 4
                pss = (ps[0], ps[1], ps[4], ps[5])[gq]
                if g2 == 0:
                    wtk = []
                    if h >= 2:
                        wtk.append(("t", t_scores_last[(i, h - 2)]))
                    elif i >= 1 and h < 2:
                        wtk.append(("t", t_scores_last[(i - 1, 14 + h)]))
                    p.dma("sync", lambda e, i=i, h=h: e.dma_start(
                        out=kcb[h % 2][:, 0:PA[i] * 128].rearrange("p (t n) -> p t n", n=128),
                        in_=kc_scr[h, 0:PA[i]].rearrange("t p n -> p t n")),
                        wait=tuple(wtk), sem="q")
                    kc_q[h] = p.v("q")
                wt = []
                if g2 == 0:
                    wt += [("q", kc_q[h]), ("q", qslab_q)]
                    if (i, h - 2) in epi_v:
                        wt.append(("v", epi_v[(i, h - 2)]))
                    elif h < 2 and (i - 1, 14 + h) in epi_v:
                        wt.append(("v", epi_v[(i - 1, 14 + h)]))
                    if h == 0 and i == 0:
                        wt.append(("v", wo_v))
                p.c("tensor", lambda tn, i=i, h=h, g2=g2, nk=nk, pss=pss: (
                    tn.matmul(pss[:, 0:nk * 128],
                              qcb[i % 2][:, h * 128:(h + 1) * 128],
                              kcb[h % 2][:, g2 * 512:g2 * 512 + nk * 128],
                              start=True, stop=False),
                    tn.matmul(pss[:, 0:nk * 128],
                              qrb[i % 2][0:128, h * 128:(h + 1) * 128],
                              kr_sb[0:128, g2 * 512:g2 * 512 + nk * 128],
                              start=False, stop=True))[-1],
                    wait=tuple(wt), sem="t")
                t_scores_last[(i, h)] = p.v("t")
                wt = [("t", p.v("t"))]
                if u - 4 in v_clamp_u:
                    wt.append(("v", v_clamp_u[u - 4]))
                p.c("scalar", lambda sc, gq=gq, nk=nk, pss=pss: sc.activation(
                    expb[gq][:, 0:nk * 128], pss[:, 0:nk * 128], AF.Exp,
                    scale=SCALE_QK), wait=tuple(wt), sem="s")
                wt = [("s", p.v("s"))]
                if u - 4 in t_tr_u:
                    wt.append(("t", t_tr_u[u - 4]))

                def clamp_fn(v, i=i, h=h, g2=g2, nk=nk, sb=gq, lo_j=lo_j, ng=ng, g_band=g_band):
                    v.tensor_scalar(expb[sb][:, 0:nk * 128], expb[sb][:, 0:nk * 128],
                                    CLAMP_HI, None, ALU.min)
                    if g2 != g_band:
                        return v.tensor_scalar(
                            prtm[sb][:, 0:nk * 128], expb[sb][:, 0:nk * 128],
                            CLAMP_LO, None, ALU.max, op1=ALU.add,
                            accum_out=rsb[:, h * 8 + g2:h * 8 + g2 + 1])
                    v.tensor_scalar(prtm[sb][:, 0:nk * 128], expb[sb][:, 0:nk * 128],
                                    CLAMP_LO, None, ALU.max)
                    last = None
                    nxt = ng
                    for t2 in range(nk):
                        s = 4 * g2 + t2
                        dst = prtm[sb][:, t2 * 128:(t2 + 1) * 128]
                        acc = rsb[:, h * 8 + g2:h * 8 + g2 + 1] if t2 == 0 \
                            else rsb[:, h * 8 + nxt:h * 8 + nxt + 1]
                        if t2 > 0:
                            nxt += 1
                        if s < lo_j:
                            last = v.tensor_scalar(dst, dst, 1.0, None, ALU.mult,
                                                   op1=ALU.add, accum_out=acc)
                        else:
                            k = 2 * i + (s - lo_j)
                            last = v.scalar_tensor_tensor(
                                dst, dst, 1.0, maskb[:, k * 128:(k + 1) * 128],
                                ALU.mult, ALU.mult, accum_out=acc)
                    return last
                p.c("vector", clamp_fn, wait=tuple(wt), sem="v")
                v_clamp_u[u] = p.v("v")

            def emit_back(u, i=i, ng=ng):
                h, g2, nk = units[u]
                gq = u % 4
                pb2 = u % 2
                avb = ps[2 + h % 2]
                wt = [("v", v_clamp_u[u])]
                if u - 2 in v_evac_u:
                    wt.append(("v", v_evac_u[u - 2]))
                p.c("tensor", lambda tn, gq=gq, pb2=pb2, nk=nk: [
                    tn.transpose(psb[pb2][:, t2 * 128:(t2 + 1) * 128],
                                 prtm[gq][:, t2 * 128:(t2 + 1) * 128], ident[:])
                    for t2 in range(nk)][-1],
                    wait=tuple(wt), sem="t")
                t_tr_u[u] = p.v("t")
                wt = [("t", t_tr_u[u])]
                if u - 4 in t_av_u:
                    wt.append(("t", t_av_u[u - 4]))
                p.c("vector", lambda v, gq=gq, pb2=pb2, nk=nk: v.tensor_copy(
                    prT[gq][:, 0:nk * 128], psb[pb2][:, 0:nk * 128]),
                    wait=tuple(wt), sem="v")
                v_evac_u[u] = p.v("v")
                p.c("tensor", lambda tn, i=i, h=h, g2=g2, nk=nk, gq=gq, avb=avb: [
                    tn.matmul(avb[:, 0:128], prT[gq][:, t2 * 128:(t2 + 1) * 128],
                              v_sb[:, (4 * g2 + t2) * 2048 + h * 128:
                                   (4 * g2 + t2) * 2048 + h * 128 + 128],
                              start=(g2 == 0 and t2 == 0),
                              stop=(4 * g2 + t2 == PA[i] - 1))
                    for t2 in range(nk)][-1],
                    wait=(("v", v_evac_u[u]),), sem="t")
                t_av_u[u] = p.v("t")
                if g2 == ng - 1:
                    p.c("vector", lambda v, h=h, avb=avb: v.tensor_copy(
                        attn_tm[:, h * 128:(h + 1) * 128], avb[:, 0:128]),
                        wait=(("t", t_av_u[u]),), sem="v")
                    epi_v[(i, h)] = p.v("v")

            for k in range(len(units) + SKEW):
                if k >= SKEW:
                    emit_back(k - SKEW)
                if k < len(units):
                    emit_front(k)

            nkb = min(4, PA[i] - 4 * g_band)
            nrs = ((PA[i] + 3) // 4) + max(0, nkb - 1)

            def slot_epi(v, i=i, nrs=nrs):
                v.drain()
                v.tensor_reduce(
                    rsb[:, 128:144].rearrange("p (a b) -> p a b", b=1),
                    rsb[:, 0:128].rearrange("p (a b) -> p a b", b=8)[:, :, 0:nrs],
                    axis=AX.X, op=ALU.add)
                v.drain()
                v.reciprocal(rsb[:, 144:160], rsb[:, 128:144])
                v.drain()
                last = None
                for h in range(H):
                    last = v.tensor_scalar(attn_tm[:, h * 128:(h + 1) * 128],
                                           attn_tm[:, h * 128:(h + 1) * 128],
                                           rsb[:, 144 + h:145 + h], None, ALU.mult)
                return last
            p.c("vector", slot_epi, wait=(("v", p.v("v")),), sem="v")

            attn_v = p.v("v")
            for kb in range(4):
                wt = [("v", attn_v)] if kb < 2 else [("v", p.v("v"))]
                p.c("tensor", lambda tn, kb=kb: [
                    tn.transpose(psb[kb % 2][:, q * 128:q * 128 + 128],
                                 attn_tm[:, (kb * 4 + q) * 128:(kb * 4 + q) * 128 + 128],
                                 ident[:]) for q in range(4)][-1],
                    wait=tuple(wt), sem="t")
                p.c("vector", lambda v, kb=kb: [
                    v.tensor_copy(attn_fm[:, (kb * 4 + q) * 128:(kb * 4 + q) * 128 + 128],
                                  psb[kb % 2][:, q * 128:q * 128 + 128]) for q in range(4)][-1],
                    wait=(("t", p.v("t")),), sem="v")
            afm_v = p.v("v")
            for ncol in range(4):
                wt = [("v", afm_v)] if ncol == 0 else \
                     ([("v", p.v("v"))] if ncol >= 2 else [])
                p.c("tensor", lambda tn, ncol=ncol: [
                    tn.matmul(ps[2 + ncol % 2][:, 0:512],
                              attn_fm[:, afc * 128:(afc + 1) * 128],
                              wo[:, afc * 2048 + ncol * 512: afc * 2048 + ncol * 512 + 512],
                              start=(afc == 0), stop=(afc == 15)) for afc in range(16)][-1],
                    wait=tuple(wt), sem="t")
                p.c("vector", lambda v, ncol=ncol: v.tensor_copy(
                    opstg[:, ncol * 512:(ncol + 1) * 512], ps[2 + ncol % 2][:, 0:512]),
                    wait=(("t", p.v("t")),) + ((("q", p.v("q")),) if ncol == 0 else ()),
                    sem="v")
            p.dma("sync", lambda e, i=i: e.dma_start(out=op_scr[i], in_=opstg[:]),
                  wait=(("v", p.v("v")),), sem="q")
        b_w = p.v("w")
        b_q = p.v("q")
        tap_out(("op_scr",), (("q", b_q),))
        p.mark["Be"] = len(p.steps)
        _open.remove(bctx); bctx.close()
        _open.remove(abctx); abctx.close()
        if end_at == "B":
            raise _EndBuild
        cctx = ExitStack()
        _open.append(cctx)
        CC = cctx.enter_context
        xno2 = CC(nc.sbuf_tensor("xno2", [128, 16 * 1024], BF16))
        ffn = CC(nc.sbuf_tensor("ffn", [128, 8 * 2048], F32))
        h_sb = CC(nc.sbuf_tensor("h_sb", [128, 16 * 1024], BF16))
        sil = [CC(nc.sbuf_tensor(f"sil{i}", [128, 512], BF16)) for i in range(2)]
        w1s = [CC(nc.sbuf_tensor(f"w1s{i}", [128, 16 * 128], BF16)) for i in range(2)]
        wgs = [CC(nc.sbuf_tensor(f"wgs{i}", [128, 16 * 128], BF16)) for i in range(2)]
        w2s = [CC(nc.sbuf_tensor(f"w2s{i}", [128, 16 * 512], BF16)) for i in range(2)]
        gwn = CC(nc.sbuf_tensor("gwn", [128, 16 * NE], BF16))
        glog = CC(nc.sbuf_tensor("glog", [128, 2 * NE + 4], F32))
        grk = CC(nc.sbuf_tensor("grk", [128, 49 + NE], F32))
        w_all = CC(nc.sbuf_tensor("w_all", [128, 8 * NE], F32))
        opin = CC(nc.sbuf_tensor("opin", [128, D], F32))
        xre = CC(nc.sbuf_tensor("xre", [128, D], F32))
        ystg = CC(nc.sbuf_tensor("ystg", [128, D], F32))

        def mk_xno2_dma(i):
            def f(e):
                rv = j_rv(e, i)
                return e.dma_start(
                    out=xno2[:].rearrange("p (dc n) -> p dc n", n=1024)[:, :, i * 128:(i + 1) * 128],
                    in_=xn_scr[ds(rv, 1)].squeeze(0).rearrange("p (dc n) -> p dc n", dc=16))
            return f
        for i in range(NO):
            p.dma("sync", mk_xno2_dma(i), wait=((("q", b_q),) if i == 0 else ()), sem="q")
        p.dma("gpsimd", lambda e: e.dma_start(
            out=gwn[:].rearrange("p (dc n) -> p dc n", n=NE),
            in_=w["gate_w"].rearrange("(dc p) n -> p dc n", p=128)),
            wait=(("q", b_q),))
        xno2_q = p.v("q")
        gw_w = p.v("w")

        for tc in range(NO):
            p.c("tensor", lambda tn, tc=tc: [
                tn.matmul(ps[4][:, 0:NE],
                          xno2[:, dc * 1024 + tc * 128: dc * 1024 + tc * 128 + 128],
                          gwn[:, dc * NE:(dc + 1) * NE],
                          start=(dc == 0), stop=(dc == 15)) for dc in range(16)][-1],
                wait=(("q", xno2_q), ("w", gw_w)) if tc == 0 else (("v", p.v("v")),), sem="t")

            def gate_fn(v, tc=tc):
                v.tensor_add(glog[:, 0:NE], ps[4][:, 0:NE], gateb[:])
                v.drain()
                v.reduce_max(glog[:, 2 * NE:2 * NE + 1], glog[:, 0:NE], axis=AX.X)
                v.drain()
                return v.tensor_scalar_mul(glog[:, 2 * NE + 1:2 * NE + 2], glog[:, 2 * NE:2 * NE + 1], -1.0)
            p.c("vector", gate_fn, wait=(("t", p.v("t")),), sem="v")
            p.c("scalar", lambda sc: sc.activation(
                glog[:, NE:2 * NE], glog[:, 0:NE], AF.Exp,
                bias=glog[:, 2 * NE + 1:2 * NE + 2],
                accum_out=glog[:, 2 * NE + 2:2 * NE + 3]),
                wait=(("v", p.v("v")),), sem="s")

            def gate2_fn(v, tc=tc):
                v.reciprocal(glog[:, 2 * NE + 3:2 * NE + 4], glog[:, 2 * NE + 2:2 * NE + 3])
                v.drain()
                v.tensor_scalar(glog[:, NE:2 * NE], glog[:, NE:2 * NE],
                                glog[:, 2 * NE + 3:2 * NE + 4], None, ALU.mult)
                v.drain()
                v.tensor_add(glog[:, 0:NE], glog[:, NE:2 * NE], biase[:])
                v.drain()
                a3 = glog[:, 0:NE].unsqueeze(1).broadcast_to([128, NE, NE])
                b3 = glog[:, 0:NE].unsqueeze(2).broadcast_to([128, NE, NE])
                v.tensor_tensor(grk[:, 0:49].rearrange("p (i j) -> p i j", i=NE), a3, b3, ALU.is_gt)
                v.drain()
                v.tensor_reduce(grk[:, 49:49 + NE],
                                grk[:, 0:49].rearrange("p (i j) -> p i j", i=NE),
                                axis=AX.X, op=ALU.add)
                v.drain()
                v.tensor_single_scalar(grk[:, 49:49 + NE], grk[:, 49:49 + NE], 2.5, ALU.is_lt)
                v.drain()
                v.tensor_mul(grk[:, 0:NE], glog[:, NE:2 * NE], grk[:, 49:49 + NE])
                v.drain()
                v.reduce_sum(glog[:, 2 * NE:2 * NE + 1], grk[:, 0:NE], axis=AX.X)
                v.drain()
                v.reciprocal(glog[:, 2 * NE + 1:2 * NE + 2], glog[:, 2 * NE:2 * NE + 1])
                v.drain()
                return v.tensor_scalar(w_all[:, tc * NE:(tc + 1) * NE], grk[:, 0:NE],
                                       glog[:, 2 * NE + 1:2 * NE + 2], None, ALU.mult)
            p.c("vector", gate2_fn, wait=(("s", p.v("s")),), sem="v")
        gate_v = p.v("v")

        up_dma_w = {}
        t_up_last = {}
        t_dn_last = {}
        v_mul_u = {}
        v_dn_u = {}
        uu_c = [0]
        dd_c = [0]
        for ei in range(8):
            shared = (ei == 0)
            e_idx = ei - 1
            w1h = w["Ws1"] if shared else w["W1"][e_idx]
            wgh = w["Wsg"] if shared else w["Wg"][e_idx]
            w2h = w["Ws2"] if shared else w["W2"][e_idx]
            for hc in range(16):
                bi = hc % 2
                wt = []
                if (ei, hc - 2) in t_up_last:
                    wt.append(("t", t_up_last[(ei, hc - 2)]))
                elif hc < 2 and (ei - 1, 14 + hc) in t_up_last:
                    wt.append(("t", t_up_last[(ei - 1, 14 + hc)]))
                p.dma("gpsimd", lambda e, w1h=w1h, hc=hc, bi=bi: e.dma_start(
                    out=w1s[bi][:].rearrange("p (b n) -> p b n", b=16),
                    in_=w1h[:, hc * 128:(hc + 1) * 128].rearrange("(b p) n -> p b n", p=128)),
                    wait=tuple(wt), sem="w")
                p.dma("gpsimd", lambda e, wgh=wgh, hc=hc, bi=bi: e.dma_start(
                    out=wgs[bi][:].rearrange("p (b n) -> p b n", b=16),
                    in_=wgh[:, hc * 128:(hc + 1) * 128].rearrange("(b p) n -> p b n", p=128)),
                    sem="w")
                up_w = p.v("w")
                for t2 in range(2):
                    uu = uu_c[0]
                    uu_c[0] += 1
                    pa, pb = (ps[0], ps[1]) if uu % 2 == 0 else (ps[2], ps[3])
                    wt = [("w", up_w)]
                    if ei == 0 and hc == 0 and t2 == 0:
                        wt.append(("q", xno2_q))
                    if uu - 2 in v_mul_u:
                        wt.append(("v", v_mul_u[uu - 2]))
                    p.c("tensor", lambda tn, hc=hc, t2=t2, bi=bi, pa=pa: [
                        tn.matmul(pa[:, 0:512],
                                  w1s[bi][:, dc * 128:(dc + 1) * 128],
                                  xno2[:, dc * 1024 + t2 * 512: dc * 1024 + t2 * 512 + 512],
                                  start=(dc == 0), stop=(dc == 15)) for dc in range(16)][-1],
                        wait=tuple(wt), sem="t")
                    t_pa = p.v("t")
                    p.c("tensor", lambda tn, hc=hc, t2=t2, bi=bi, pb=pb: [
                        tn.matmul(pb[:, 0:512],
                                  wgs[bi][:, dc * 128:(dc + 1) * 128],
                                  xno2[:, dc * 1024 + t2 * 512: dc * 1024 + t2 * 512 + 512],
                                  start=(dc == 0), stop=(dc == 15)) for dc in range(16)][-1],
                        sem="t")
                    t_up_last[(ei, hc)] = p.v("t")
                    wt = [("t", t_pa)]
                    if uu - 2 in v_mul_u:
                        wt.append(("v", v_mul_u[uu - 2]))
                    p.c("scalar", lambda sc, uu=uu, pa=pa: sc.activation(
                        sil[uu % 2][:], pa[:, 0:512], AF.Silu),
                        wait=tuple(wt), sem="s")
                    p.c("vector", lambda v, hc=hc, t2=t2, uu=uu, pb=pb: v.tensor_mul(
                        h_sb[:, hc * 1024 + t2 * 512: hc * 1024 + t2 * 512 + 512],
                        pb[:, 0:512], sil[uu % 2][:]),
                        wait=(("s", p.v("s")), ("t", t_up_last[(ei, hc)])), sem="v")
                    v_mul_u[uu] = p.v("v")
            up_v = p.v("v")
            for dcol in range(4):
                bi = dcol % 2
                wt = []
                if (ei, dcol - 2) in t_dn_last:
                    wt.append(("t", t_dn_last[(ei, dcol - 2)]))
                elif dcol < 2 and (ei - 1, 2 + dcol) in t_dn_last:
                    wt.append(("t", t_dn_last[(ei - 1, 2 + dcol)]))
                p.dma("gpsimd", lambda e, w2h=w2h, dcol=dcol, bi=bi: e.dma_start(
                    out=w2s[bi][:].rearrange("p (b n) -> p b n", b=16),
                    in_=w2h[:, dcol * 512:(dcol + 1) * 512].rearrange("(b p) n -> p b n", p=128)),
                    wait=tuple(wt), sem="w")
                dn_w = p.v("w")
                for tc in range(NO):
                    dd = dd_c[0]
                    dd_c[0] += 1
                    pdn = ps[4 + dd % 2]
                    wt = [("w", dn_w)]
                    if dcol == 0 and tc == 0:
                        wt.append(("v", up_v))
                    if dd - 2 in v_dn_u:
                        wt.append(("v", v_dn_u[dd - 2]))
                    p.c("tensor", lambda tn, dcol=dcol, tc=tc, bi=bi, pdn=pdn: [
                        tn.matmul(pdn[:, 0:512],
                                  h_sb[:, hc * 1024 + tc * 128: hc * 1024 + tc * 128 + 128],
                                  w2s[bi][:, hc * 512:(hc + 1) * 512],
                                  start=(hc == 0), stop=(hc == 15)) for hc in range(16)][-1],
                        wait=tuple(wt), sem="t")
                    t_dn_last[(ei, dcol)] = p.v("t")
                    if shared:
                        p.c("vector", lambda v, dcol=dcol, tc=tc, pdn=pdn: v.tensor_copy(
                            ffn[:, tc * 2048 + dcol * 512: tc * 2048 + dcol * 512 + 512],
                            pdn[:, 0:512]),
                            wait=(("t", p.v("t")),), sem="v")
                    else:
                        p.c("vector", lambda v, dcol=dcol, tc=tc, pdn=pdn, e_idx=e_idx: v.scalar_tensor_tensor(
                            ffn[:, tc * 2048 + dcol * 512: tc * 2048 + dcol * 512 + 512],
                            pdn[:, 0:512], w_all[:, tc * NE + e_idx: tc * NE + e_idx + 1],
                            ffn[:, tc * 2048 + dcol * 512: tc * 2048 + dcol * 512 + 512],
                            ALU.mult, ALU.add),
                            wait=(("t", p.v("t")),), sem="v")
                    v_dn_u[dd] = p.v("v")
        ffn_v = p.v("v")

        def mk_xre_c(i):
            def f(e):
                rv = jx_rv(e, i)
                return e.dma_start(out=xre[:], in_=x_h[ds(rv, 128), :])
            return f
        for tc in range(NO):
            p.dma("sync", lambda e, tc=tc: e.dma_start(out=opin[:], in_=op_scr[tc]),
                  wait=((("v", ffn_v),) if tc == 0 else (("o", p.v("o")),)), sem="q")
            p.dma("sync", mk_xre_c(tc), sem="x")
            p.c("vector", lambda v, tc=tc: v.tensor_add(
                ystg[:], ffn[:, tc * 2048:(tc + 1) * 2048], opin[:]),
                wait=(("q", p.v("q")), ("o", p.v("o"))), sem="v")
            p.c("vector", lambda v: v.tensor_add(ystg[:], ystg[:], xre[:]),
                wait=(("x", p.v("x")),), sem="v")
            p.dma("sync", lambda e, tc=tc: e.dma_start(out=y_h[tc], in_=ystg[:]),
                  wait=(("v", p.v("v")),), sem="o")
        _open.remove(cctx); cctx.close()
        p.step("sync", None, wait=(("o", p.v("o")),))
      except _EndBuild:
        for _st in reversed(_open):
            _st.close()
      if True:
        for sname in ("x", "w", "w2", "t", "v", "s", "q", "o"):
            if p.v(sname):
                p.step("sync", None, wait=((sname, p.v(sname)),))

        mk = getattr(p, "mark", {})
        a1e = mk.get("A1e", 10**9); a2e = mk.get("A2e", 10**9); be = mk.get("Be", 10**9)

        def phase_of(idx):
            if idx < a1e:
                return "phA"
            if idx < a2e:
                return "phA2"
            if idx < be:
                return "phB"
            return "phC"

        @block.sync
        def _(e):
            p.emit("sync", e, sems, marks=phase_of)

        @block.gpsimd
        def _(e):
            p.emit("gpsimd", e, sems, marks=phase_of)

        @block.tensor
        def _(e):
            p.emit("tensor", e, sems, marks=phase_of)

        @block.vector
        def _(e):
            p.emit("vector", e, sems, marks=phase_of)

        @block.scalar
        def _(e):
            p.emit("scalar", e, sems, marks=phase_of)

    return nc


def make_consts():
    ident = np.eye(128, dtype=ml_dtypes.bfloat16)
    ones = np.ones((128, 128), dtype=np.float32)
    causal = np.tril(np.ones((128, 128), np.float32))
    zeros = np.zeros((128, 128), dtype=np.float32)
    masks = np.concatenate([ones, zeros], axis=1)
    pos = (np.arange(NT)[None, :] * 128 + np.arange(128)[:, None]).astype(np.float32)
    return ident, masks, pos, (ones, causal, zeros)


def make_mask_band(own):
    ident, masks, pos, (ones, causal, zeros) = make_consts()
    band = np.zeros((128, 16 * 128), np.float32)
    for i, j in enumerate(own):
        lo = min(OWN_EVEN[i], OWN_ODD[i])
        for b in range(2):
            s = lo + b
            tile = causal if s == j else (ones if s < j else zeros)
            band[:, (2 * i + b) * 128:(2 * i + b + 1) * 128] = tile
    return band


def make_trig_tables():
    import jax
    import jax.numpy as jnp
    with jax.default_device(jax.devices("cpu")[0]):
        inv_freq = 10000.0 ** (-jnp.arange(0, 2 * 32, 2, dtype=jnp.float32) / (2 * 32))
        ang = jnp.arange(S).astype(jnp.float32)[:, None] * inv_freq[None, :]
        cos, sin = jnp.cos(ang), jnp.sin(ang)
    def lay(a):
        return np.ascontiguousarray(
            np.asarray(a, np.float32).reshape(16, 128, 32).transpose(1, 2, 0).reshape(128, 512))
    return lay(sin), lay(cos)


def kernel(**inputs):
    x = np.asarray(inputs["L0_attn_x"], np.float32)
    ident, masks, pos, _ = make_consts()
    base = {
        "W_DQ": inputs["L0_attn_W_DQ"], "W_UQ": inputs["L0_attn_W_UQ"],
        "W_QR": inputs["L0_attn_W_QR"], "W_DKV": inputs["L0_attn_W_DKV"],
        "W_UK": inputs["L0_attn_W_UK"], "W_UV": inputs["L0_attn_W_UV"],
        "W_KR": inputs["L0_attn_W_KR"], "W_O": inputs["L0_attn_W_O"],
        "W1": inputs["L0_ffn_W1"], "Wg": inputs["L0_ffn_Wg"], "W2": inputs["L0_ffn_W2"],
        "Ws1": inputs["L0_ffn_Ws1"], "Wsg": inputs["L0_ffn_Wsg"], "Ws2": inputs["L0_ffn_Ws2"],
        "gate_w": inputs["L0_ffn_gate_w"],
        "gate_b_rep": np.ascontiguousarray(np.tile(np.asarray(inputs["L0_ffn_gate_b"], np.float32)[None, :], (128, 1))),
        "biases_rep": np.ascontiguousarray(np.tile(np.asarray(inputs["L0_ffn_biases"], np.float32)[None, :], (128, 1))),
        "ident": ident, "pos": pos,
    }
    base = {k: np.ascontiguousarray(np.asarray(v, dtype=(ml_dtypes.bfloat16 if k == "ident" else np.float32)))
            for k, v in base.items()}
    nc = build_nc()
    sin_tab, cos_tab = make_trig_tables()
    in_maps = []
    for c in range(8):
        b, par = c // 2, c % 2
        own = OWN_EVEN if par == 0 else OWN_ODD
        m = dict(base)
        m["x"] = np.ascontiguousarray(x[b])
        m["own_j"] = np.asarray([own], np.int32)
        m["mask_band"] = make_mask_band(own).astype(ml_dtypes.bfloat16)
        m["zpad"] = np.zeros((64, 16 * 128), ml_dtypes.bfloat16)
        m["sin_t"] = sin_tab
        m["cos_t"] = cos_tab
        in_maps.append(m)
    res = run_bass_kernel_spmd(nc, in_maps, core_ids=list(range(8)))
    out = np.zeros((B, S, D), np.float32)
    for c in range(8):
        b, par = c // 2, c % 2
        own = OWN_EVEN if par == 0 else OWN_ODD
        yc = res.results[c]["y"]
        for i, j in enumerate(own):
            out[b, j * 128:(j + 1) * 128, :] = yc[i]
    return out


def _mask_sel(own):
    sel = []
    for i, j in enumerate(own):
        for s in range(PA[i]):
            sel.append(0 if s < j else (128 if s == j else 256))
    return sel
